# version 06
# baseline (speedup 1.0000x reference)
import math
from contextlib import ExitStack

import numpy as np
import ml_dtypes

import concourse.bass as bass
import concourse.mybir as mybir
from concourse.bass_utils import run_bass_kernel_spmd

F32 = mybir.dt.float32
BF16 = mybir.dt.bfloat16
I32 = mybir.dt.int32
AF = mybir.ActivationFunctionType
ALU = mybir.AluOpType
AX = mybir.AxisListType.X
bfdt = ml_dtypes.bfloat16

NB, S, D = 2, 4096, 4096
NH, HD = 32, 128
INTER, RANK = 11008, 1024
NPT = 32
NI = 86
SCALE = 1.0 / math.sqrt(HD)
EPS = 1e-5
NEG = -1e30
SLOPES = [2.0 ** (-(i + 1) / 4.0) for i in range(NH)]

BKS = [0, 1536, 2560, 3584, 4096]
OMAX = 1536


def _chunks_of(n0, n1):
    out = []
    o = n0
    while n1 - o >= 1024:
        out.append((o, 1024))
        o += 1024
    if n1 - o:
        out.append((o, n1 - o))
        o = n1
    return out



def _pack_weights(Wq, Wkv, Wo, U1, V1, U2, V2, b1, b2):
    p = {}
    p["u1p"] = np.ascontiguousarray(
        U1.reshape(32, 128, 8, 128).transpose(2, 1, 0, 3).reshape(8, 128, 4096))
    v1t = V1.reshape(8, 128, 172, 128).transpose(2, 1, 0, 3).reshape(172, 128, 1024)
    u2t = U2.reshape(86, 128, 8, 128).reshape(86, 128, 1024)
    p["v1u2p"] = np.ascontiguousarray(
        np.concatenate([v1t[:86], v1t[86:], u2t], axis=2))
    p["v2p"] = np.ascontiguousarray(
        V2.reshape(8, 128, 32, 128).transpose(2, 1, 0, 3).reshape(32, 128, 1024))
    p["wqp"] = np.ascontiguousarray(
        Wq.reshape(32, 128, 16, 256).transpose(2, 1, 0, 3).reshape(16, 128, 8192))
    p["wop"] = np.ascontiguousarray(
        Wo.reshape(32, 128, 16, 256).transpose(2, 1, 0, 3).reshape(16, 128, 8192))
    p["wkvp"] = np.ascontiguousarray(
        Wkv.reshape(32, 128, 256).transpose(1, 0, 2).reshape(128, 8192))
    p["b1p"] = np.ascontiguousarray(b1.reshape(172, 128).T)
    p["b2p"] = np.ascontiguousarray(b2.reshape(32, 128).T)
    return p


def _variant_meta(j):
    t0, t1 = BKS[j] // 128, BKS[j + 1] // 128
    tiles = list(range(t0, t1))
    if j == 0:
        tiles = [0, 4, 1, 5, 2, 6, 3, 7, 8, 9, 10, 11]
    ext = [t // 4 + 1 for t in tiles]
    nsc = sum(ext)
    scidx = {}
    k = 0
    for s, e in enumerate(ext):
        for c in range(e):
            scidx[(s, c)] = k
            k += 1
    return dict(tiles=tiles, ext=ext, nsc=nsc, scidx=scidx, t0=t0,
                O=BKS[j + 1] - BKS[j], P=BKS[j + 1])


VM = [_variant_meta(j) for j in range(4)]
NSC_MAX = max(v["nsc"] for v in VM)


def _consts_shared():
    c = {}
    dq = np.arange(128, dtype=np.float32)[:, None]
    dk = np.arange(512, dtype=np.float32)[None, :]
    c["r0"] = np.ascontiguousarray((dk - dq).astype(np.float32))
    m4 = [np.where(dk - dq <= 128.0 * m, 0.0, NEG).astype(np.float32)
          for m in range(4)]
    c["mask4"] = np.ascontiguousarray(np.concatenate(m4, axis=1).astype(bfdt))
    c["idn_bf"] = np.eye(128, dtype=bfdt)
    c["idn_f32"] = np.eye(128, dtype=np.float32)
    c["ones_bf"] = np.ones((128, 1), dtype=bfdt)
    c["ones_f32r"] = np.ones((1, 128), dtype=np.float32)
    return c


def _otab_row(j):
    v = VM[j]
    row = np.zeros((1, NH * NSC_MAX), np.float32)
    for h in range(NH):
        for s, t in enumerate(v["tiles"]):
            for cc in range(v["ext"][s]):
                row[0, h * NSC_MAX + v["scidx"][(s, cc)]] = \
                    SLOPES[h] * (512.0 * cc - 128.0 * t)
    return row



class Sems:
    def __init__(self, nc, es, names):
        self.h = {n: es.enter_context(nc.semaphore(n)) for n in names}
        self.n = {n: 0 for n in names}

    def snap(self):
        return dict(self.n)

    def restore(self, st):
        self.n = dict(st)

    def inc(self, instr, name, k=1):
        instr.then_inc(self.h[name], k)
        self.n[name] += k
        return self.n[name]

    def __getitem__(self, name):
        return self.h[name]


class VPad:

    def __init__(self, eng):
        self._e = eng

    def __getattr__(self, name):
        a = getattr(self._e, name)
        if callable(a):
            def wrap(*args, **kw):
                r = a(*args, **kw)
                self._e.nop()
                return r
            return wrap
        return a


class GBlock:

    def __init__(self, blk, vregs, key, case):
        self.blk, self.vregs, self.key, self.case = blk, vregs, key, case

    def _mk(self, name):
        def deco(f):
            if self.key is None:
                body = f
            else:
                def body(e):
                    with e.If_eq(self.vregs[name][self.key], self.case):
                        f(VPad(e) if name == "vector" else e)
            getattr(self.blk, name)(body)
        return deco

    @property
    def gpsimd(self):
        return self._mk("gpsimd")

    @property
    def sync(self):
        return self._mk("sync")

    @property
    def scalar(self):
        return self._mk("scalar")

    @property
    def vector(self):
        return self._mk("vector")

    @property
    def tensor(self):
        return self._mk("tensor")


class GGate:

    def __init__(self, nc, vregs, key, case):
        self.nc, self.vregs, self.key, self.case = nc, vregs, key, case

    def Block(self):
        class _Ctx:
            def __init__(s2):
                s2.cm = self.nc.Block()

            def __enter__(s2):
                blk = s2.cm.__enter__()
                return GBlock(blk, self.vregs, self.key, self.case)

            def __exit__(s2, *a):
                return s2.cm.__exit__(*a)
        return _Ctx()


def _common_bufs(nc, es, pfx=""):
    sb = lambda n, sh, dt: es.enter_context(nc.sbuf_tensor(pfx + n, sh, dt))
    bufs = {
        "xst": [sb(f"xst{i}", [128, 1024], F32) for i in range(2)],
        "xbf": [sb(f"xbf{i}", [128, 1024], BF16) for i in range(2)],
        "xsq": [sb(f"xsq{i}", [128, 1024], BF16) for i in range(2)],
        "row": sb("row", [1, 3072], F32),
        "mb_sb": sb("mb", [128, 1024], F32),
        "ib_sb": sb("ib", [128, 1024], F32),
        "u1st": [sb(f"u1st{i}", [128, 512], F32) for i in range(3)],
        "u1bf": [sb(f"u1bf{i}", [128, 512], BF16) for i in range(2)],
        "wst": [sb(f"wst{i}", [128, 1536], F32) for i in range(3)],
        "wbf": [sb(f"wbf{i}", [128, 3072], BF16) for i in range(2)],
        "v2st": [sb(f"v2st{i}", [128, 512], F32) for i in range(2)],
        "v2bf": [sb(f"v2bf{i}", [128, 1024], BF16) for i in range(2)],
        "sg_sb": sb("sg", [128, 1024], F32),
        "ub_sb": sb("ub", [128, 1024], F32),
        "act_bf": [sb(f"act{i}", [128, 1024], BF16) for i in range(2)],
        "b1p": None, "b2h": None, "ones_bf": None, "ones_f32r": None,
    }
    bufs["xst2"] = bufs["xst"]
    bufs["t_sb"] = [bufs["sg_sb"], bufs["ub_sb"]]
    return bufs


LN_SEMS = ["lnld0", "lnld1", "lncast", "lnmm", "lnfin", "lnap",
           "lnld20", "lnld21"]
FFN_SEMS = ["fu1ld0", "fu1ld1", "fu1c", "fu1u", "fpev",
            "fwld0", "fwld1", "fwc",
            "fhmm", "fhdr", "fact", "fzmm", "fzd",
            "fv2ld0", "fv2ld1", "fv2c",
            "fomm", "ft", "fxld0", "fxld1", "fxadd", "fst0", "fst1", "fzcast"]
KV_SEMS = ["kvld0", "kvld1", "kvc", "kvmm", "kvev", "yst"]
ATT_SEMS = ["vktr", "vkev", "knsq", "knmm", "knev",
            "qwld0", "qwld1", "qwc", "qwu", "qev", "qyld0", "qyld1",
            "qnsq", "qnmm", "qnev", "bch",
            "scmm", "scv", "sexp", "strc", "ptev", "spv",
            "nsum", "nrow1", "nrec", "nrb2", "nrbc", "ndone", "relw",
            "bldr", "bldv",
            "wwld0", "wwld1", "wwc", "wwu", "wxld0", "wxld1", "wxadd",
            "wst20", "wst21"]


def _emit_ln(nc, gb, sm, bufs, src_dram, y_sb, T, so, pre=None):
    H = T // 512
    with ExitStack() as pse:
        psf = lambda n, sh: pse.enter_context(nc.psum_tensor(n + _uid(), sh, F32))
        sx = [psf(f"lnsx{i}", [1, 512]) for i in range(H)]
        sq = [psf(f"lnsq{i}", [1, 512]) for i in range(H)]
        bc = [psf(f"lnbc{i}", [128, 512]) for i in range(2 * H)]
        xst, xbf, xsq = bufs["xst"], bufs["xbf"], bufs["xsq"]
        row, mb_sb, ib_sb = bufs["row"], bufs["mb_sb"], bufs["ib_sb"]
        ones_bf, ones_f32r = bufs["ones_bf"], bufs["ones_f32r"]
        b_ld = [sm.n["lnld0"], sm.n["lnld1"]]
        b_ld2 = [sm.n["lnld20"], sm.n["lnld21"]]
        b_cast = sm.n["lncast"]; b_mm = sm.n["lnmm"]
        b_fin = sm.n["lnfin"]; b_ap = sm.n["lnap"]

        with gb.Block() as blk:
            @blk.gpsimd
            def _(g):
                for pt in range(NPT):
                    if pt == 0 and pre is not None:
                        for pn, pv in pre:
                            g.wait_ge(sm[pn], pv)
                    if pt >= 2:
                        g.wait_ge(sm["lncast"], b_cast + pt - 1)
                    sm.inc(g.dma_start(out=xst[pt % 2][:, 0:T],
                                       in_=src_dram[pt, :, so:so + T]),
                           f"lnld{pt % 2}", 16)

            @blk.scalar
            def _(s):
                for pt in range(NPT):
                    s.wait_ge(sm[f"lnld{pt % 2}"], b_ld[pt % 2] + 16 * (pt // 2 + 1))
                    if pt >= 2:
                        s.wait_ge(sm["lnmm"], b_mm + pt - 1)
                    s.activation(xbf[pt % 2][:, 0:T], xst[pt % 2][:, 0:T], AF.Copy)
                    sm.inc(s.activation(xsq[pt % 2][:, 0:T], xst[pt % 2][:, 0:T],
                                        AF.Square), "lncast")
                s.wait_ge(sm["lnmm"], b_mm + NPT)
                for h2 in range(H):
                    last = s.activation(row[:, h2 * 512:(h2 + 1) * 512], sx[h2][:, :],
                                        AF.Copy, scale=1.0 / 4096.0)
                sm.inc(last, "lnfin")
                s.wait_ge(sm["lnfin"], b_fin + 4)
                sm.inc(s.activation(row[:, 2048:2048 + T], row[:, 2048:2048 + T],
                                    AF.Sqrt), "lnfin")
                s.wait_ge(sm["lnmm"], b_mm + NPT + 1)
                for h2 in range(H):
                    s.activation(mb_sb[:, h2 * 512:(h2 + 1) * 512], bc[h2][:, :],
                                 AF.Copy)
                    last = s.activation(ib_sb[:, h2 * 512:(h2 + 1) * 512],
                                        bc[H + h2][:, :], AF.Copy)
                sm.inc(last, "lnfin")
                for pt in range(NPT):
                    if pt < 2:
                        s.wait_ge(sm["lncast"], b_cast + 31 + pt)
                    else:
                        s.wait_ge(sm["lnap"], b_ap + 2 * (pt - 1))
                    sm.inc(s.dma_start(out=xst[pt % 2][:, 0:T],
                                       in_=src_dram[pt, :, so:so + T]),
                           f"lnld2{pt % 2}", 16)

            @blk.tensor
            def _(t):
                for pt in range(NPT):
                    t.wait_ge(sm["lncast"], b_cast + pt + 1)
                    for h2 in range(H):
                        c0 = h2 * 512
                        t.matmul(sx[h2][:, :], ones_bf[:, :],
                                 xbf[pt % 2][:, c0:c0 + 512],
                                 start=(pt == 0), stop=(pt == NPT - 1))
                        last = t.matmul(sq[h2][:, :], ones_bf[:, :],
                                        xsq[pt % 2][:, c0:c0 + 512],
                                        start=(pt == 0), stop=(pt == NPT - 1))
                    sm.inc(last, "lnmm")
                t.wait_ge(sm["lnfin"], b_fin + 6)
                for h2 in range(H):
                    t.matmul(bc[h2][:, :], ones_f32r[:, :],
                             row[:, h2 * 512:(h2 + 1) * 512], start=True, stop=True)
                    last = t.matmul(bc[H + h2][:, :], ones_f32r[:, :],
                                    row[:, 1024 + h2 * 512:1024 + (h2 + 1) * 512],
                                    start=True, stop=True)
                sm.inc(last, "lnmm")

            @blk.vector
            def _(v):
                v.wait_ge(sm["lnfin"], b_fin + 1)
                sm.inc(v.tensor_mul(row[:, 1024:1024 + T], row[:, 0:T], row[:, 0:T]),
                       "lnfin")
                v.wait_ge(sm["lnfin"], b_fin + 2)
                for h2 in range(H):
                    last = v.scalar_tensor_tensor(
                        row[:, 2048 + h2 * 512:2048 + (h2 + 1) * 512], sq[h2][:, :],
                        1.0 / 4096.0, row[:, 1024 + h2 * 512:1024 + (h2 + 1) * 512],
                        ALU.mult, ALU.subtract)
                sm.inc(last, "lnfin")
                v.wait_ge(sm["lnfin"], b_fin + 3)
                sm.inc(v.tensor_scalar(row[:, 2048:2048 + T], row[:, 2048:2048 + T],
                                       EPS, None, ALU.add), "lnfin")
                v.wait_ge(sm["lnfin"], b_fin + 5)
                sm.inc(v.reciprocal(row[:, 1024:1024 + T], row[:, 2048:2048 + T]),
                       "lnfin")
                v.wait_ge(sm["lnfin"], b_fin + 7)
                for pt in range(NPT):
                    v.wait_ge(sm[f"lnld2{pt % 2}"], b_ld2[pt % 2] + 16 * (pt // 2 + 1))
                    sm.inc(v.tensor_sub(xst[pt % 2][:, 0:T], xst[pt % 2][:, 0:T],
                                        mb_sb[:, 0:T]), "lnap")
                    v.wait_ge(sm["lnap"], b_ap + 2 * pt + 1)
                    sm.inc(v.tensor_mul(y_sb[:, pt * 1024:pt * 1024 + T],
                                        xst[pt % 2][:, 0:T], ib_sb[:, 0:T]), "lnap")


_UID = [0]


def _uid():
    _UID[0] += 1
    return f"_{_UID[0]}"


def _emit_ffn(nc, gb, sm, bufs, y_sb, p_sb, zacc, xsrc, xdst, wd, T, so, do,
              is_final):
    H = T // 512
    with ExitStack() as pse:
        psf = lambda n, sh: pse.enter_context(nc.psum_tensor(n + _uid(), sh, F32))
        q4 = [psf(f"fq{i}", [128, 512]) for i in range(2 * H)]
        zp = [psf(f"fzp{i}", [128, 1024]) for i in range(2)]
        u1st, u1bf = bufs["u1st"], bufs["u1bf"]
        wst, wbf = bufs["wst"], bufs["wbf"]
        v2st, v2bf = bufs["v2st"], bufs["v2bf"]
        sg_sb, ub_sb = bufs["sg_sb"], bufs["ub_sb"]
        act_bf, t_sb, xst2 = bufs["act_bf"], bufs["t_sb"], bufs["xst2"]
        b1p, b2h = bufs["b1p"], bufs["b2h"]
        b = {k: sm.n[k] for k in FFN_SEMS}

        def q4i(par, h2):
            return q4[H * par + h2] if H == 2 else q4[par]

        with gb.Block() as blk:
            @blk.sync
            def _(s):
                for n in range(0, 64, 2):
                    pm, kc = n // 8, n % 8
                    if n >= 2:
                        s.wait_ge(sm["fu1c"], b["fu1c"] + n - 1)
                    sm.inc(s.dma_start(out=u1st[0][:, :],
                                       in_=wd["u1p"][pm, :, kc * 512:(kc + 1) * 512]),
                           "fu1ld0", 16)
                for i in range(NI):
                    n = 2 * i
                    if n >= 2:
                        s.wait_ge(sm["fwc"], b["fwc"] + n - 1)
                    sm.inc(s.dma_start(
                        out=wst[0][:, :],
                        in_=wd["v1u2p"][i, :, 0:1536]), "fwld0", 16)
                for om in range(32):
                    for hf in range(2):
                        n = 2 * om + hf
                        if n >= 2:
                            s.wait_ge(sm["fv2c"], b["fv2c"] + n - 1)
                        sm.inc(s.dma_start(
                            out=v2st[n % 2][:, :],
                            in_=wd["v2p"][om, :, hf * 512:(hf + 1) * 512]),
                               f"fv2ld{n % 2}", 16)

            @blk.gpsimd
            def _(g):
                for os_ in range(32):
                    g.wait_ge(sm["fxadd"], b["fxadd"] + os_ + 1)
                    sm.inc(g.dma_start(out=xdst[os_, :, do:do + T],
                                       in_=xst2[os_ % 2][:, 0:T]),
                           f"fst{os_ % 2}", 16)
                if is_final:
                    g.wait_ge(sm["fst0"], b["fst0"] + 16 * 16)
                    g.wait_ge(sm["fst1"], b["fst1"] + 16 * 16)

            @blk.scalar
            def _(s):
                def u1odd(n):
                    pm, kc = n // 8, n % 8
                    sm.inc(s.dma_start(out=u1st[1 + (n // 2) % 2][:, :],
                                       in_=wd["u1p"][pm, :, kc * 512:(kc + 1) * 512]),
                           "fu1ld1", 16)

                def u1cast(n):
                    s.wait_ge(sm[f"fu1ld{n % 2}"],
                              b[f"fu1ld{n % 2}"] + 16 * (n // 2 + 1))
                    if n >= 2:
                        s.wait_ge(sm["fu1u"], b["fu1u"] + n - 1)
                    usrc = u1st[0] if n % 2 == 0 else u1st[1 + (n // 2) % 2]
                    sm.inc(s.activation(u1bf[n % 2][:, :], usrc[:, :],
                                        AF.Copy), "fu1c")
                    if n + 2 < 64 and (n + 2) % 2 == 1:
                        u1odd(n + 2)

                u1odd(1)
                u1cast(0)
                u1cast(1)
                for pm in range(8):
                    for kc in range(8):
                        n = pm * 8 + kc
                        if n + 2 < 64:
                            u1cast(n + 2)
                    if pm >= 1:
                        s.wait_ge(sm["fu1u"], b["fu1u"] + 8 * pm)
                        for h2 in range(H):
                            c0 = (pm - 1) * 1024 + h2 * 512
                            last = s.activation(p_sb[:, c0:c0 + 512],
                                                q4i((pm - 1) % 2, h2)[:, :], AF.Copy)
                        sm.inc(last, "fpev")
                s.wait_ge(sm["fu1u"], b["fu1u"] + 64)
                for h2 in range(H):
                    c0 = 7 * 1024 + h2 * 512
                    last = s.activation(p_sb[:, c0:c0 + 512], q4i(1, h2)[:, :],
                                        AF.Copy)
                sm.inc(last, "fpev")

                def wodd(n):
                    sm.inc(s.dma_start(
                        out=wst[1 + (n // 2) % 2][:, :],
                        in_=wd["v1u2p"][n // 2, :, 1536:3072]), "fwld1", 16)

                def wcast(n):
                    i2, hf = n // 2, n % 2
                    s.wait_ge(sm[f"fwld{n % 2}"],
                              b[f"fwld{n % 2}"] + 16 * (n // 2 + 1))
                    if i2 >= 2 and hf == 0:
                        s.wait_ge(sm["fzmm"], b["fzmm"] + 8 * (i2 - 1))
                    wsrc = wst[0] if hf == 0 else wst[1 + (n // 2) % 2]
                    sm.inc(s.activation(wbf[i2 % 2][:, hf * 1536:(hf + 1) * 1536],
                                        wsrc[:, :], AF.Copy), "fwc")
                    if n + 2 < 2 * NI and (n + 2) % 2 == 1:
                        wodd(n + 2)

                wodd(1)
                for n in range(4):
                    wcast(n)
                for i in range(NI):
                    s.wait_ge(sm["fhmm"], b["fhmm"] + i + 1)
                    if i >= 1:
                        s.wait_ge(sm["fact"], b["fact"] + i)
                    for h2 in range(H):
                        c0 = h2 * 512
                        s.activation(sg_sb[:, c0:c0 + 512], q4i(0, h2)[:, :], AF.Silu,
                                     bias=b1p[:, i:i + 1])
                        last = s.activation(ub_sb[:, c0:c0 + 512], q4i(1, h2)[:, :],
                                            AF.Identity, bias=b1p[:, 86 + i:87 + i])
                    sm.inc(last, "fhdr")
                    for hf in range(2):
                        n = 2 * (i + 2) + hf
                        if n < 2 * NI:
                            wcast(n)
                s.wait_ge(sm["fzd"], b["fzd"] + NI * 8)
                if T == 1024:
                    sm.inc(s.activation(p_sb[:, :], zacc[:, :], AF.Copy),
                           "fzcast")
                else:
                    for zm in range(8):
                        last = s.activation(p_sb[:, zm * 1024:zm * 1024 + T],
                                            zacc[:, zm * 1024:zm * 1024 + T],
                                            AF.Copy)
                    sm.inc(last, "fzcast")

                def v2cast(n):
                    om2, hf = n // 2, n % 2
                    s.wait_ge(sm[f"fv2ld{n % 2}"],
                              b[f"fv2ld{n % 2}"] + 16 * (n // 2 + 1))
                    if om2 >= 2 and hf == 0:
                        s.wait_ge(sm["fomm"], b["fomm"] + om2 - 1)
                    sm.inc(s.activation(v2bf[om2 % 2][:, hf * 512:(hf + 1) * 512],
                                        v2st[n % 2][:, :], AF.Copy), "fv2c")

                for n in range(4):
                    v2cast(n)
                for om in range(32):
                    if om >= 2:
                        s.wait_ge(sm[f"fst{om % 2}"],
                                  b[f"fst{om % 2}"] + 16 * (om // 2))
                    sm.inc(s.dma_start(out=xst2[om % 2][:, 0:T],
                                       in_=xsrc[om, :, so:so + T]),
                           f"fxld{om % 2}", 16)
                    s.wait_ge(sm["fomm"], b["fomm"] + om + 1)
                    if om >= 2:
                        s.wait_ge(sm["fxadd"], b["fxadd"] + om - 1)
                    for h2 in range(H):
                        last = s.activation(t_sb[om % 2][:, h2 * 512:(h2 + 1) * 512],
                                            q4i(om % 2, h2)[:, :], AF.Identity,
                                            scale=0.5, bias=b2h[:, om:om + 1])
                    sm.inc(last, "ft")
                    for hf in range(2):
                        n = 2 * (om + 2) + hf
                        if n < 64:
                            v2cast(n)

            @blk.tensor
            def _(t):
                for pm in range(8):
                    if pm >= 2:
                        t.wait_ge(sm["fpev"], b["fpev"] + pm - 1)
                    for kc in range(8):
                        n = pm * 8 + kc
                        t.wait_ge(sm["fu1c"], b["fu1c"] + n + 1)
                        for jj in range(4):
                            k = kc * 4 + jj
                            for h2 in range(H):
                                last = t.matmul(q4i(pm % 2, h2)[:, :],
                                                u1bf[n % 2][:, jj * 128:(jj + 1) * 128],
                                                y_sb[:, k * 1024 + h2 * 512:k * 1024 + h2 * 512 + 512],
                                                start=(k == 0), stop=(k == 31))
                        sm.inc(last, "fu1u")
                for i in range(NI + 1):
                    if i < NI:
                        t.wait_ge(sm["fwc"], b["fwc"] + 2 * (i + 1))
                        t.wait_ge(sm["fhdr"], b["fhdr"] + i)
                        if i == 0:
                            t.wait_ge(sm["fpev"], b["fpev"] + 8)
                        for part in range(2):
                            for k in range(8):
                                for h2 in range(H):
                                    cw = part * 1024 + k * 128
                                    last = t.matmul(
                                        q4i(part, h2)[:, :],
                                        wbf[i % 2][:, cw:cw + 128],
                                        p_sb[:, k * 1024 + h2 * 512:k * 1024 + h2 * 512 + 512],
                                        start=(k == 0), stop=(k == 7))
                        sm.inc(last, "fhmm")
                    if i >= 1:
                        ii = i - 1
                        t.wait_ge(sm["fact"], b["fact"] + ii + 1)
                        for zm in range(8):
                            g = ii * 8 + zm
                            if g >= 2:
                                t.wait_ge(sm["fzd"], b["fzd"] + g - 1)
                            for h2 in range(H):
                                last = t.matmul(
                                    zp[g % 2][:, h2 * 512:(h2 + 1) * 512],
                                    wbf[ii % 2][:, 2048 + zm * 128:2048 + (zm + 1) * 128],
                                    act_bf[ii % 2][:, h2 * 512:(h2 + 1) * 512],
                                    start=True, stop=True)
                            sm.inc(last, "fzmm")
                for om in range(32):
                    t.wait_ge(sm["fv2c"], b["fv2c"] + 2 * (om + 1))
                    if om == 0:
                        t.wait_ge(sm["fzcast"], b["fzcast"] + 1)
                        t.wait_ge(sm["fhdr"], b["fhdr"] + NI)
                    if om >= 2:
                        t.wait_ge(sm["ft"], b["ft"] + om - 1)
                    for k in range(8):
                        for h2 in range(H):
                            last = t.matmul(q4i(om % 2, h2)[:, :],
                                            v2bf[om % 2][:, k * 128:(k + 1) * 128],
                                            p_sb[:, k * 1024 + h2 * 512:k * 1024 + h2 * 512 + 512],
                                            start=(k == 0), stop=(k == 7))
                    sm.inc(last, "fomm")

            @blk.vector
            def _(v):
                for i in range(NI):
                    if i >= 1:
                        ii = i - 1
                        for zm in range(8):
                            g = ii * 8 + zm
                            v.wait_ge(sm["fzmm"], b["fzmm"] + g + 1)
                            dst = zacc[:, zm * 1024:zm * 1024 + T]
                            if ii == 0:
                                sm.inc(v.tensor_copy(dst, zp[g % 2][:, 0:T]), "fzd")
                            else:
                                sm.inc(v.tensor_add(dst, dst, zp[g % 2][:, 0:T]),
                                       "fzd")
                    v.wait_ge(sm["fhdr"], b["fhdr"] + i + 1)
                    sm.inc(v.tensor_mul(act_bf[i % 2][:, 0:T], sg_sb[:, 0:T],
                                        ub_sb[:, 0:T]), "fact")
                ii = NI - 1
                for zm in range(8):
                    g = ii * 8 + zm
                    v.wait_ge(sm["fzmm"], b["fzmm"] + g + 1)
                    dst = zacc[:, zm * 1024:zm * 1024 + T]
                    sm.inc(v.tensor_add(dst, dst, zp[g % 2][:, 0:T]), "fzd")
                for om in range(32):
                    v.wait_ge(sm["ft"], b["ft"] + om + 1)
                    v.wait_ge(sm[f"fxld{om % 2}"],
                              b[f"fxld{om % 2}"] + 16 * (om // 2 + 1))
                    sm.inc(v.tensor_add(xst2[om % 2][:, 0:T], xst2[om % 2][:, 0:T],
                                        t_sb[om % 2][:, 0:T]), "fxadd")


def _emit_kv(nc, gb, sm, bufs, y_sb, WKVP, KVB, YB, T, so):
    H = T // 512
    with ExitStack() as pse:
        kvp = [pse.enter_context(nc.psum_tensor(f"kvp{i}" + _uid(), [128, 512], F32))
               for i in range(2 * H)]
        kvw, kvwb = bufs["u1st"], bufs["u1bf"]
        kv_bf = bufs["act_bf"]
        b = {k: sm.n[k] for k in KV_SEMS + ["lnap"]}

        def kvpi(m2, h2):
            return kvp[H * m2 + h2] if H == 2 else kvp[m2]

        with gb.Block() as blk:
            @blk.gpsimd
            def _(g):
                for pt in range(NPT):
                    g.wait_ge(sm["lnap"], b["lnap"])
                    sm.inc(g.dma_start(out=YB[pt, :, so:so + T],
                                       in_=y_sb[:, pt * 1024:pt * 1024 + T]),
                           "yst", 16)
                g.wait_ge(sm["kvev"], b["kvev"] + 1)
                for m2 in range(2):
                    sm.inc(g.dma_start(out=KVB[m2, :, so:so + T],
                                       in_=kv_bf[m2][:, 0:T]), "yst", 16)
                g.wait_ge(sm["yst"], b["yst"] + 16 * (NPT + 2))

            @blk.sync
            def _(s):
                for k in range(32):
                    if k >= 2:
                        s.wait_ge(sm["kvc"], b["kvc"] + k - 1)
                    sm.inc(s.dma_start(out=kvw[k % 2][:, 0:256],
                                       in_=WKVP[:, k * 256:(k + 1) * 256]),
                           f"kvld{k % 2}", 16)

            @blk.scalar
            def _(s):
                for k in range(32):
                    s.wait_ge(sm[f"kvld{k % 2}"],
                              b[f"kvld{k % 2}"] + 16 * (k // 2 + 1))
                    if k >= 2:
                        s.wait_ge(sm["kvmm"], b["kvmm"] + k - 1)
                    sm.inc(s.activation(kvwb[k % 2][:, 0:256], kvw[k % 2][:, 0:256],
                                        AF.Copy), "kvc")
                s.wait_ge(sm["kvmm"], b["kvmm"] + 32)
                for m2 in range(2):
                    for h2 in range(H):
                        last = s.activation(kv_bf[m2][:, h2 * 512:(h2 + 1) * 512],
                                            kvpi(m2, h2)[:, :], AF.Copy)
                sm.inc(last, "kvev")

            @blk.tensor
            def _(t):
                for k in range(32):
                    t.wait_ge(sm["kvc"], b["kvc"] + k + 1)
                    for m2 in range(2):
                        for h2 in range(H):
                            last = t.matmul(
                                kvpi(m2, h2)[:, :],
                                kvwb[k % 2][:, m2 * 128:(m2 + 1) * 128],
                                y_sb[:, k * 1024 + h2 * 512:k * 1024 + h2 * 512 + 512],
                                start=(k == 0), stop=(k == 31))
                    sm.inc(last, "kvmm")


def _emit_qproj(nc, gb, sm, att, WQP, YB, T, so, lo, OC):
    H = T // 512
    y2s, q_sb, wqs, wqb = att["y2s"], att["q_sb"], att["wqs"], att["wqb"]
    with ExitStack() as pse:
        qp = [pse.enter_context(nc.psum_tensor(f"qp{i}" + _uid(), [128, 512], F32))
              for i in range(4 * H)]
        b = {k: sm.n[k] for k in ("qwld0", "qwld1", "qwc", "qwu", "qev",
                                  "qyld0", "qyld1")}

        def qpi(par, m2, h2):
            return qp[2 * H * par + H * m2 + h2]

        with gb.Block() as blk:
            @blk.gpsimd
            def _(g):
                for pt in range(NPT):
                    sm.inc(g.dma_start(out=y2s[:, pt * 1024:pt * 1024 + T],
                                       in_=YB[pt, :, so:so + T]),
                           f"qyld{pt % 2}", 16)

            @blk.sync
            def _(s):
                for n in range(0, 512, 2):
                    if n >= 2:
                        s.wait_ge(sm["qwc"], b["qwc"] + n - 1)
                    sm.inc(s.dma_start(
                        out=wqs[0][:, :],
                        in_=WQP[n // 32, :, (n % 32) * 256:(n % 32 + 1) * 256]),
                           "qwld0", 16)

            @blk.scalar
            def _(s):
                def qodd(n):
                    sm.inc(s.dma_start(
                        out=wqs[1 + (n // 2) % 2][:, :],
                        in_=WQP[n // 32, :, (n % 32) * 256:(n % 32 + 1) * 256]),
                           "qwld1", 16)

                qodd(1)
                for g2 in range(16):
                    for k in range(32):
                        n = g2 * 32 + k
                        s.wait_ge(sm[f"qwld{n % 2}"],
                                  b[f"qwld{n % 2}"] + 16 * (n // 2 + 1))
                        if n >= 2:
                            s.wait_ge(sm["qwu"], b["qwu"] + n - 1)
                        qsrc = wqs[0] if n % 2 == 0 else wqs[1 + (n // 2) % 2]
                        sm.inc(s.activation(wqb[n % 2][:, :], qsrc[:, :],
                                            AF.Copy), "qwc")
                        if n + 2 < 512 and (n + 2) % 2 == 1:
                            qodd(n + 2)
                    s.wait_ge(sm["qwu"], b["qwu"] + 32 * (g2 + 1))
                    for m2 in range(2):
                        hh = 2 * g2 + m2
                        for h2 in range(H):
                            c0 = hh * OC + lo + h2 * 512
                            last = s.activation(q_sb[:, c0:c0 + 512],
                                                qpi(g2 % 2, m2, h2)[:, :], AF.Copy,
                                                scale=SCALE)
                    sm.inc(last, "qev")

            @blk.tensor
            def _(t):
                t.wait_ge(sm["qyld0"], b["qyld0"] + 16 * 16)
                t.wait_ge(sm["qyld1"], b["qyld1"] + 16 * 16)
                for g2 in range(16):
                    if g2 >= 2:
                        t.wait_ge(sm["qev"], b["qev"] + g2 - 1)
                    for k in range(32):
                        n = g2 * 32 + k
                        t.wait_ge(sm["qwc"], b["qwc"] + n + 1)
                        for m2 in range(2):
                            for h2 in range(H):
                                last = t.matmul(
                                    qpi(g2 % 2, m2, h2)[:, :],
                                    wqb[n % 2][:, m2 * 128:(m2 + 1) * 128],
                                    y2s[:, k * 1024 + h2 * 512:k * 1024 + h2 * 512 + 512],
                                    start=(k == 0), stop=(k == 31))
                        sm.inc(last, "qwu")
    return


def _emit_wo(nc, gb, sm, att, WOP, X1SRC, XB, T, so_abs, lo, OC):
    H = T // 512
    q_sb, wqs, wqb, xsw = att["q_sb"], att["wqs"], att["wqb"], att["xsw"]
    with ExitStack() as pse:
        op = [pse.enter_context(nc.psum_tensor(f"wop{i}" + _uid(), [128, 512], F32))
              for i in range(4 * H)]
        b = {k: sm.n[k] for k in ("wwld0", "wwld1", "wwc", "wwu", "wxld0",
                                  "wxld1", "wxadd", "wst20", "wst21")}

        def opi(par, m2, h2):
            return op[2 * H * par + H * m2 + h2]

        with gb.Block() as blk:
            @blk.sync
            def _(s):
                for n in range(0, 512, 2):
                    if n >= 2:
                        s.wait_ge(sm["wwc"], b["wwc"] + n - 1)
                    sm.inc(s.dma_start(
                        out=wqs[0][:, :],
                        in_=WOP[n // 32, :, (n % 32) * 256:(n % 32 + 1) * 256]),
                           "wwld0", 16)

            @blk.scalar
            def _(s):
                def wodd(n):
                    sm.inc(s.dma_start(
                        out=wqs[1 + (n // 2) % 2][:, :],
                        in_=WOP[n // 32, :, (n % 32) * 256:(n % 32 + 1) * 256]),
                           "wwld1", 16)

                wodd(1)
                for g2 in range(16):
                    for k in range(32):
                        n = g2 * 32 + k
                        s.wait_ge(sm[f"wwld{n % 2}"],
                                  b[f"wwld{n % 2}"] + 16 * (n // 2 + 1))
                        if n >= 2:
                            s.wait_ge(sm["wwu"], b["wwu"] + n - 1)
                        wsrc = wqs[0] if n % 2 == 0 else wqs[1 + (n // 2) % 2]
                        sm.inc(s.activation(wqb[n % 2][:, :], wsrc[:, :],
                                            AF.Copy), "wwc")
                        if n + 2 < 512 and (n + 2) % 2 == 1:
                            wodd(n + 2)

            @blk.tensor
            def _(t):
                for g2 in range(16):
                    if g2 >= 1:
                        t.wait_ge(sm["wxadd"], b["wxadd"] + 2 * (g2 - 1))
                    for k in range(32):
                        n = g2 * 32 + k
                        t.wait_ge(sm["wwc"], b["wwc"] + n + 1)
                        for m2 in range(2):
                            for h2 in range(H):
                                last = t.matmul(
                                    opi(g2 % 2, m2, h2)[:, :],
                                    wqb[n % 2][:, m2 * 128:(m2 + 1) * 128],
                                    q_sb[:, k * OC + lo + h2 * 512:k * OC + lo + h2 * 512 + 512],
                                    start=(k == 0), stop=(k == 31))
                        sm.inc(last, "wwu")

            @blk.gpsimd
            def _(g):
                for om in range(32):
                    if om == 0:
                        g.wait_ge(sm["wst20"], b["wst20"])
                        g.wait_ge(sm["wst21"], b["wst21"])
                    if om >= 2:
                        g.wait_ge(sm[f"wst2{om % 2}"],
                                  b[f"wst2{om % 2}"] + 16 * (om // 2))
                    sm.inc(g.dma_start(out=xsw[om % 2][:, 0:T],
                                       in_=X1SRC[om, :, so_abs:so_abs + T]),
                           f"wxld{om % 2}", 16)
                    g.wait_ge(sm["wxadd"], b["wxadd"] + om + 1)
                    sm.inc(g.dma_start(out=XB[om, :, so_abs:so_abs + T],
                                       in_=xsw[om % 2][:, 0:T]),
                           f"wst2{om % 2}", 16)

            @blk.vector
            def _(v):
                for om in range(32):
                    g2, m2 = om // 2, om % 2
                    v.wait_ge(sm["wwu"], b["wwu"] + 32 * (g2 + 1))
                    v.wait_ge(sm[f"wxld{om % 2}"],
                              b[f"wxld{om % 2}"] + 16 * (om // 2 + 1))
                    for h2 in range(H):
                        last = v.tensor_add(xsw[om % 2][:, h2 * 512:(h2 + 1) * 512],
                                            xsw[om % 2][:, h2 * 512:(h2 + 1) * 512],
                                            opi(g2 % 2, m2, h2)[:, :])
                    sm.inc(last, "wxadd")


def _emit_vkeys(nc, gb, sm, att, P):
    nkt_all = P // 128
    kt_sb, vt_sb, vk_sb, idnb = att["kt"], att["vt"], att["vk"], att["idnb"]
    KVB = att["KVB"]
    with ExitStack() as pse:
        tp = [pse.enter_context(nc.psum_tensor(f"vtp{i}" + _uid(), [128, 128], BF16))
              for i in range(2)]
        b_l1 = sm.n["qyld1"]
        b_tr = sm.n["vktr"]; b_ev = sm.n["vkev"]
        with gb.Block() as blk:
            @blk.gpsimd
            def _(g):
                sm.inc(g.dma_start(out=kt_sb[:, 0:P], in_=KVB[0, :, 0:P]),
                       "qyld0", 16)
                sm.inc(g.dma_start(out=vt_sb[:, 0:P], in_=KVB[1, :, 0:P]),
                       "qyld1", 16)

            @blk.tensor
            def _(t):
                t.wait_ge(sm["qyld1"], b_l1 + 16)
                for n in range(nkt_all):
                    if n >= 2:
                        t.wait_ge(sm["vkev"], b_ev + n - 1)
                    sm.inc(t.transpose(tp[n % 2][:, :],
                                       vt_sb[:, n * 128:(n + 1) * 128],
                                       idnb[:, :]), "vktr")

            @blk.vector
            def _(v2):
                for n in range(nkt_all):
                    v2.wait_ge(sm["vktr"], b_tr + n + 1)
                    sm.inc(v2.tensor_copy(vk_sb[:, n * 128:(n + 1) * 128],
                                          tp[n % 2][:, :]), "vkev")


def _emit_kmax(nc, gb, sm, att, P):
    kt_sb, qsq, nrow, bsc = att["kt"], att["qsq"], att["nrow"], att["bsc"]
    onesb = att["onesb"]
    chs = []
    o = 0
    while o < P:
        w = min(1024, P - o)
        chs.append((o, w))
        o += w
    nch = len(chs)
    mc = [w // 512 for (_, w) in chs]
    cmm = [0]
    for m in mc:
        cmm.append(cmm[-1] + m)
    cev = [0]
    for m in mc:
        cev.append(cev[-1] + m + 1)
    with ExitStack() as pse:
        knp = pse.enter_context(nc.psum_tensor("knp" + _uid(), [1, 512], F32))
        b_sq = sm.n["knsq"]; b_mm = sm.n["knmm"]; b_ev = sm.n["knev"]
        kt_rdy = sm.n["qyld0"]
        with gb.Block() as blk:
            @blk.scalar
            def _(s):
                s.wait_ge(sm["qyld0"], kt_rdy)
                for ch, (o2, w) in enumerate(chs):
                    if ch >= 1:
                        s.wait_ge(sm["knmm"], b_mm + cmm[ch])
                    sm.inc(s.activation(qsq[:, 0:w], kt_sb[:, o2:o2 + w],
                                        AF.Square), "knsq")

            @blk.tensor
            def _(t):
                prev = None
                for ch in range(nch):
                    t.wait_ge(sm["knsq"], b_sq + ch + 1)
                    for h2 in range(mc[ch]):
                        if prev is not None:
                            t.wait_ge(sm["knev"], b_ev + prev)
                        sm.inc(t.matmul(knp[:, :], onesb[:, :],
                                        qsq[:, h2 * 512:(h2 + 1) * 512],
                                        start=True, stop=True), "knmm")
                        prev = cev[ch] + h2 + 1

            @blk.vector
            def _(v2):
                for ch, (o2, w) in enumerate(chs):
                    for h2 in range(mc[ch]):
                        v2.wait_ge(sm["knmm"], b_mm + cmm[ch] + h2 + 1)
                        if h2 == 0 and ch >= 1:
                            v2.wait_ge(sm["knev"], b_ev + cev[ch])
                        sm.inc(v2.tensor_copy(nrow[:, h2 * 512:(h2 + 1) * 512],
                                              knp[:, :]), "knev")
                    v2.wait_ge(sm["knev"], b_ev + cev[ch] + mc[ch])
                    sm.inc(v2.reduce_max(bsc[:, 64 + ch:65 + ch], nrow[:, 0:w],
                                         axis=AX), "knev")
                v2.wait_ge(sm["knev"], b_ev + cev[nch])
                sm.inc(v2.reduce_max(bsc[:, 0:1], bsc[:, 64:64 + nch], axis=AX),
                       "knev")


def _emit_norms(nc, gb, sm, att, vj):
    v = VM[vj]
    OC, nsc = v["O"], v["nsc"]
    nq = OC // 512
    q_sb, qsq, nrow, bsc = att["q_sb"], att["qsq"], att["nrow"], att["bsc"]
    bcol, ba, otr = att["bcol"], att["ba"], att["otr"]
    onesb, onesfr = att["onesb"], att["onesfr"]
    with ExitStack() as pse:
        np1 = pse.enter_context(nc.psum_tensor("np1" + _uid(), [1, 512], F32))
        bbp = pse.enter_context(nc.psum_tensor("bbp" + _uid(), [128, 32], F32))
        batp = [pse.enter_context(nc.psum_tensor(f"bat{i}" + _uid(),
                                                 [128, NSC_MAX], F32))
                for i in range(2)]
        b_sq = sm.n["qnsq"]; b_mm = sm.n["qnmm"]; b_ev = sm.n["qnev"]
        b_bc = sm.n["bch"]; b_br = sm.n["bldr"]; b_bv = sm.n["bldv"]
        with gb.Block() as blk:
            @blk.scalar
            def _(s):
                for h in range(NH):
                    if h >= 1:
                        s.wait_ge(sm["qnmm"], b_mm + nq * h)
                    sm.inc(s.activation(qsq[:, 0:OC], q_sb[:, h * OC:(h + 1) * OC],
                                        AF.Square), "qnsq")
                s.wait_ge(sm["bch"], b_bc + 1)
                sm.inc(s.activation(bsc[:, 32:64], bsc[:, 32:64], AF.Sqrt),
                       "bch")
                s.wait_ge(sm["bch"], b_bc + 3)
                sm.inc(s.activation(bcol[:, :], bbp[:, :], AF.Copy), "bch")

            @blk.tensor
            def _(t):
                for h in range(NH):
                    t.wait_ge(sm["qnsq"], b_sq + h + 1)
                    for h2 in range(nq):
                        if h2 > 0:
                            t.wait_ge(sm["qnev"], b_ev + (nq + 1) * h + h2)
                        elif h >= 1:
                            t.wait_ge(sm["qnev"], b_ev + (nq + 1) * h - 1)
                        sm.inc(t.matmul(np1[:, :], onesb[:, :],
                                        qsq[:, h2 * 512:(h2 + 1) * 512],
                                        start=True, stop=True), "qnmm")
                t.wait_ge(sm["bch"], b_bc + 2)
                sm.inc(t.matmul(bbp[:, :], onesfr[:, :], bsc[:, 32:64],
                                start=True, stop=True), "bch")
                for h in range(NH):
                    if h >= 2:
                        t.wait_ge(sm["bldv"], b_bv + h - 1)
                    sm.inc(t.matmul(batp[h % 2][:, 0:nsc], onesfr[:, :],
                                    otr[0:1, h * NSC_MAX:h * NSC_MAX + nsc],
                                    start=True, stop=True), "bldr")

            @blk.vector
            def _(v2):
                for h in range(NH):
                    for h2 in range(nq):
                        v2.wait_ge(sm["qnmm"], b_mm + nq * h + h2 + 1)
                        if h2 == 0 and h >= 1:
                            v2.wait_ge(sm["qnev"], b_ev + (nq + 1) * h)
                        sm.inc(v2.tensor_copy(nrow[:, h2 * 512:(h2 + 1) * 512],
                                              np1[:, :]), "qnev")
                    v2.wait_ge(sm["qnev"], b_ev + (nq + 1) * h + nq)
                    sm.inc(v2.reduce_max(bsc[:, 1 + h:2 + h], nrow[:, 0:OC],
                                         axis=AX), "qnev")
                v2.wait_ge(sm["qnev"], b_ev + (nq + 1) * NH)
                sm.inc(v2.tensor_scalar(bsc[:, 32:64], bsc[:, 1:33],
                                        bsc[:, 0:1], None, ALU.mult), "bch")
                v2.wait_ge(sm["bch"], b_bc + 4)
                for h in range(NH):
                    v2.wait_ge(sm["bldr"], b_br + h + 1)
                    sm.inc(v2.tensor_scalar(ba[:, h * NSC_MAX:h * NSC_MAX + nsc],
                                            batp[h % 2][:, 0:nsc],
                                            bcol[:, h:h + 1], None,
                                            ALU.subtract), "bldv")


def _emit_einsum(nc, gb, sm, att, vj):
    v = VM[vj]
    tiles, ext, nsc, scidx = v["tiles"], v["ext"], v["nsc"], v["scidx"]
    OC, t0 = v["O"], v["t0"]
    nslots = len(tiles)
    kt_sb, vk_sb, q_sb = att["kt"], att["vk"], att["q_sb"]
    r0_sb, mk4_sb, idnb, idnf = att["r0"], att["mk4"], att["idnb"], att["idnf"]
    onesfr = att["onesfr"]
    relh_sb, rm4_sb = att["relh"], att["rm4"]
    sc2_sb, pr_sb, pt_sb = att["sc2"], att["pr"], att["pt"]
    sums_sb, ssum_sb, srow_sb, rb_sb = (att["sums"], att["ssum"], att["srow"],
                                        att["rb"])
    ba = att["ba"]

    chunks = []
    slot_of = {}
    g = 0
    for h in range(NH):
        for s in range(nslots):
            for c in range(ext[s]):
                chunks.append((h, s, c))
                if c == ext[s] - 1:
                    slot_of[g] = (h, s, h * nslots + s)
                g += 1
    NG = g

    with ExitStack() as pse:
        scp = [pse.enter_context(nc.psum_tensor(f"scp{i}" + _uid(), [128, 512], F32))
               for i in range(2)]
        ptp = [pse.enter_context(nc.psum_tensor(f"ptp{i}" + _uid(), [128, 512], BF16))
               for i in range(2)]
        pvp = [pse.enter_context(nc.psum_tensor(f"pvp{i}" + _uid(), [128, 128], F32))
               for i in range(2)]
        nrp = pse.enter_context(nc.psum_tensor("nrp" + _uid(), [1, 128], F32))
        rbp = pse.enter_context(nc.psum_tensor("rbp" + _uid(), [128, 128], F32))
        b_scmm = sm.n["scmm"]; b_scv = sm.n["scv"]
        b_sexp = sm.n["sexp"]; b_strc = sm.n["strc"]
        b_ptev = sm.n["ptev"]; b_spv = sm.n["spv"]
        b_nsum = sm.n["nsum"]; b_nrow1 = sm.n["nrow1"]
        b_nrec = sm.n["nrec"]; b_nrb2 = sm.n["nrb2"]
        b_nrbc = sm.n["nrbc"]; b_ndone = sm.n["ndone"]
        with gb.Block() as blk:
            @blk.tensor
            def _(t):
                for gg in range(NG + 6):
                    if gg < NG:
                        h, s, c = chunks[gg]
                        if gg >= 2:
                            t.wait_ge(sm["scv"], b_scv + gg - 1)
                        tp_ = (tiles[s] - t0) * 128
                        sm.inc(t.matmul(
                            scp[gg % 2][:, :],
                            q_sb[:, h * OC + tp_:h * OC + tp_ + 128],
                            kt_sb[:, c * 512:(c + 1) * 512],
                            start=True, stop=True), "scmm")
                    gp = gg - 2
                    if 0 <= gp < NG:
                        t.wait_ge(sm["sexp"], b_sexp + gp + 1)
                        if gp >= 2:
                            t.wait_ge(sm["ptev"], b_ptev + gp - 1)
                        for bbk in range(4):
                            last = t.transpose(
                                ptp[gp % 2][:, bbk * 128:(bbk + 1) * 128],
                                pr_sb[gp % 4][:, bbk * 128:(bbk + 1) * 128],
                                idnb[:, :])
                        sm.inc(last, "strc")
                        if gp in slot_of:
                            h2_, s2_, sl = slot_of[gp]
                            t.wait_ge(sm["ptev"], b_ptev + gp + 1)
                            if sl >= 2:
                                t.wait_ge(sm["ndone"], b_ndone + sl - 1)
                            nkt = 4 * ext[s2_]
                            for kt in range(nkt):
                                last = t.matmul(pvp[sl % 2][:, :],
                                                vk_sb[:, kt * 128:(kt + 1) * 128],
                                                pt_sb[:, kt * 128:(kt + 1) * 128],
                                                start=(kt == 0),
                                                stop=(kt == nkt - 1))
                            sm.inc(last, "spv")
                            t.wait_ge(sm["nsum"], b_nsum + sl + 1)
                            if sl >= 1:
                                t.wait_ge(sm["nrec"], b_nrec + sl)
                            sm.inc(t.matmul(nrp[:, :], ssum_sb[:, :], idnf[:, :],
                                            start=True, stop=True), "nrow1")
                    gm = gg - 3
                    if 0 <= gm < NG and gm in slot_of:
                        _h, _s, sl = slot_of[gm]
                        t.wait_ge(sm["nrec"], b_nrec + sl + 1)
                        if sl >= 1:
                            t.wait_ge(sm["nrbc"], b_nrbc + sl)
                        sm.inc(t.matmul(rbp[:, :], onesfr[:, :], srow_sb[:, :],
                                        start=True, stop=True), "nrb2")

            @blk.vector
            def _(v2):
                for gg in range(NG + 6):
                    if gg < NG:
                        h, s, c = chunks[gg]
                        if s == 0 and c == 0:
                            if h >= 1:
                                v2.wait_ge(sm["scv"], b_scv + nsc * h)
                            for m in range(4):
                                v2.scalar_tensor_tensor(
                                    rm4_sb[:, m * 512:(m + 1) * 512], r0_sb[:, :],
                                    float(SLOPES[h]),
                                    mk4_sb[:, m * 512:(m + 1) * 512],
                                    ALU.mult, ALU.add)
                            sm.inc(v2.tensor_scalar(relh_sb[:, :], r0_sb[:, :],
                                                    float(SLOPES[h]), None,
                                                    ALU.mult), "relw")
                            v2.wait_ge(sm["relw"], sm.n["relw"])
                        v2.wait_ge(sm["scmm"], b_scmm + gg + 1)
                        if gg >= 2:
                            v2.wait_ge(sm["sexp"], b_sexp + gg - 1)
                        diag = (c == ext[s] - 1)
                        rel = (rm4_sb[:, (tiles[s] % 4) * 512:
                                      (tiles[s] % 4 + 1) * 512]
                               if diag else relh_sb[:, :])
                        sm.inc(v2.scalar_tensor_tensor(
                            sc2_sb[gg % 2][:, :], scp[gg % 2][:, :], 1.0,
                            rel, ALU.mult, ALU.add), "scv")
                    gp = gg - 2
                    if 0 <= gp < NG:
                        h2_, s2_, c2_ = chunks[gp]
                        v2.wait_ge(sm["strc"], b_strc + gp + 1)
                        if c2_ == 0 and (h2_ * nslots + s2_) >= 1:
                            v2.wait_ge(sm["spv"], b_spv + h2_ * nslots + s2_)
                        sm.inc(v2.tensor_copy(pt_sb[:, c2_ * 512:(c2_ + 1) * 512],
                                              ptp[gp % 2][:, :]), "ptev")
                        if gp in slot_of:
                            _h, _s, sl = slot_of[gp]
                            v2.wait_ge(sm["sexp"], b_sexp + gp + 1)
                            if sl >= 1:
                                v2.wait_ge(sm["nrow1"], b_nrow1 + sl)
                            i0 = (h2_ % 2) * NSC_MAX + scidx[(s2_, 0)]
                            sm.inc(v2.reduce_sum(ssum_sb[:, :],
                                                 sums_sb[:, i0:i0 + ext[s2_]],
                                                 axis=AX), "nsum")
                    gr = gg - 3
                    if 0 <= gr < NG and gr in slot_of:
                        _h, _s, sl = slot_of[gr]
                        v2.wait_ge(sm["nrow1"], b_nrow1 + sl + 1)
                        if sl >= 1:
                            v2.wait_ge(sm["nrb2"], b_nrb2 + sl)
                        sm.inc(v2.reciprocal(srow_sb[:, :], nrp[:, :]), "nrec")
                    gf = gg - 5
                    if 0 <= gf < NG and gf in slot_of:
                        h2_, s2_, sl = slot_of[gf]
                        v2.wait_ge(sm["nrbc"], b_nrbc + sl + 1)
                        v2.wait_ge(sm["spv"], b_spv + sl + 1)
                        c0 = h2_ * OC + (tiles[s2_] - t0) * 128
                        sm.inc(v2.tensor_mul(q_sb[:, c0:c0 + 128],
                                             pvp[sl % 2][:, :], rb_sb[:, :]),
                               "ndone")

            @blk.scalar
            def _(s_):
                for gg in range(NG + 6):
                    if gg < NG:
                        h, s, c = chunks[gg]
                        if s == 0 and c == 0 and h >= 2:
                            s_.wait_ge(sm["nsum"], b_nsum + nslots * (h - 1))
                        s_.wait_ge(sm["scv"], b_scv + gg + 1)
                        if gg >= 4:
                            s_.wait_ge(sm["strc"], b_strc + gg - 3)
                        idx = scidx[(s, c)]
                        scol = (h % 2) * NSC_MAX + idx
                        sm.inc(s_.activation(
                            pr_sb[gg % 4][:, :], sc2_sb[gg % 2][:, :], AF.Exp,
                            bias=ba[:, h * NSC_MAX + idx:h * NSC_MAX + idx + 1],
                            accum_out=sums_sb[:, scol:scol + 1]), "sexp")
                    gb_ = gg - 4
                    if 0 <= gb_ < NG and gb_ in slot_of:
                        _h, _s, sl = slot_of[gb_]
                        s_.wait_ge(sm["nrb2"], b_nrb2 + sl + 1)
                        if sl >= 1:
                            s_.wait_ge(sm["ndone"], b_ndone + sl)
                        sm.inc(s_.activation(rb_sb[:, :], rbp[:, :], AF.Copy),
                               "nrbc")



ENGS = ["sync", "gpsimd", "vector", "scalar", "tensor"]


def build_fused():
    nc = bass.Bass("TRN2", target_bir_lowering=False)
    X = nc.dram_tensor("x", [NPT, 128, S], F32, kind="ExternalInput")
    U1 = nc.dram_tensor("u1p", [8, 128, 4096], F32, kind="ExternalInput")
    V1U2 = nc.dram_tensor("v1u2p", [86, 128, 3072], F32, kind="ExternalInput")
    V2 = nc.dram_tensor("v2p", [32, 128, 1024], F32, kind="ExternalInput")
    WQP = nc.dram_tensor("wqp", [16, 128, 8192], F32, kind="ExternalInput")
    WOP = nc.dram_tensor("wop", [16, 128, 8192], F32, kind="ExternalInput")
    WKVP = nc.dram_tensor("wkvp", [128, 8192], F32, kind="ExternalInput")
    B1P = nc.dram_tensor("b1p", [128, 172], F32, kind="ExternalInput")
    B2P = nc.dram_tensor("b2p", [128, 32], F32, kind="ExternalInput")
    R0 = nc.dram_tensor("r0", [128, 512], F32, kind="ExternalInput")
    MK4 = nc.dram_tensor("mask4", [128, 2048], BF16, kind="ExternalInput")
    IDB = nc.dram_tensor("idn_bf", [128, 128], BF16, kind="ExternalInput")
    IDF = nc.dram_tensor("idn_f32", [128, 128], F32, kind="ExternalInput")
    ONB = nc.dram_tensor("ones_bf", [128, 1], BF16, kind="ExternalInput")
    ONF = nc.dram_tensor("ones_f32r", [1, 128], F32, kind="ExternalInput")
    OTR = nc.dram_tensor("otr", [1, NH * NSC_MAX], F32, kind="ExternalInput")
    VID = nc.dram_tensor("vid", [1, 4], I32, kind="ExternalInput")
    XB = nc.dram_tensor("xb", [NPT, 128, S], F32, kind="Internal")
    YB = nc.dram_tensor("yb", [NPT, 128, S], BF16, kind="Internal")
    KVB = nc.dram_tensor("kvb", [2, 128, S], BF16, kind="Internal")
    OUT = nc.dram_tensor("out", [NPT, 128, OMAX], F32, kind="ExternalOutput")
    wd = {"u1p": U1, "v1u2p": V1U2, "v2p": V2}

    with ExitStack() as es0:
        sm = Sems(nc, es0, ["i0"] + LN_SEMS + FFN_SEMS + KV_SEMS + ATT_SEMS)
        sb0 = lambda n, sh, dt: es0.enter_context(nc.sbuf_tensor(n, sh, dt))
        vid_sb = sb0("vid_sb", [1, 4], I32)
        b1p_sb = sb0("b1ps", [128, 172], F32)
        b2p_sb = sb0("b2ps", [128, 32], F32)
        b2h_sb = sb0("b2hs", [128, 32], F32)
        onesb = sb0("onesb", [128, 1], BF16)
        onesfr = sb0("onesfr", [1, 128], F32)

        vregs = {en: {} for en in ENGS}

        with nc.Block() as blk:
            @blk.gpsimd
            def _(g):
                for dst, src in ((vid_sb, VID), (b1p_sb, B1P), (b2p_sb, B2P),
                                 (onesb, ONB), (onesfr, ONF)):
                    sm.inc(g.dma_start(out=dst[:, :], in_=src[:, :]), "i0", 16)

            def mkbody(en):
                def body(e):
                    e.wait_ge(sm["i0"], 80)
                    for key, col in (("vid", 0), ("g1", 1), ("g2", 2)):
                        r = e.alloc_register(f"{en}_{key}")
                        e.reg_load(r, vid_sb[0:1, col:col + 1])
                        vregs[en][key] = r
                getattr(blk, en)(body)

            for en in ENGS:
                mkbody(en)

            @blk.vector
            def _(v):
                v.tensor_scalar(b2h_sb[:, :], b2p_sb[:, :], 0.5, None, ALU.mult)

        def fill_common(bufs):
            bufs["b1p"] = b1p_sb
            bufs["b2h"] = b2h_sb
            bufs["ones_bf"] = onesb
            bufs["ones_f32r"] = onesfr

        def emit_prefix_chunk(gg, bufs, y_sb, p_sb, zacc, so, T, first):
            pre = (None if first else
                   [("fst0", sm.n["fst0"]), ("fst1", sm.n["fst1"])])
            _emit_ln(nc, gg, sm, bufs, X, y_sb, T, so, pre=pre)
            _emit_ffn(nc, gg, sm, bufs, y_sb, p_sb, zacc, X, XB, wd, T, so, so,
                      False)
            pre2 = [("fst0", sm.n["fst0"]), ("fst1", sm.n["fst1"])]
            _emit_ln(nc, gg, sm, bufs, XB, y_sb, T, so, pre=pre2)
            _emit_kv(nc, gg, sm, bufs, y_sb, WKVP, KVB, YB, T, so)

        with ExitStack() as esP:
            bufs = _common_bufs(nc, esP, "p_")
            fill_common(bufs)
            y_sb = esP.enter_context(nc.sbuf_tensor("p_y", [128, NPT * 1024], BF16))
            p_sb = esP.enter_context(nc.sbuf_tensor("p_p", [128, 8192], BF16))
            zacc = esP.enter_context(nc.sbuf_tensor("p_z", [128, 8192], F32))
            emit_prefix_chunk(GGate(nc, vregs, None, None), bufs, y_sb, p_sb,
                              zacc, 0, 1024, True)
            s1 = sm.snap()
            emit_prefix_chunk(GGate(nc, vregs, "g1", 1), bufs, y_sb, p_sb,
                              zacc, 1024, 1024, False)
            s2 = sm.snap()
            emit_prefix_chunk(GGate(nc, vregs, "g2", 1), bufs, y_sb, p_sb,
                              zacc, 2048, 1024, False)
            s3 = sm.snap()
            emit_prefix_chunk(GGate(nc, vregs, "vid", 3), bufs, y_sb, p_sb,
                              zacc, 3072, 1024, False)
            s4 = sm.snap()
            sm.restore(s1)
            emit_prefix_chunk(GGate(nc, vregs, "vid", 0), bufs, y_sb, p_sb,
                              zacc, 1024, 512, False)
            e0 = sm.snap()
            sm.restore(s2)
            emit_prefix_chunk(GGate(nc, vregs, "vid", 1), bufs, y_sb, p_sb,
                              zacc, 2048, 512, False)
            e1 = sm.snap()
            sm.restore(s3)
            emit_prefix_chunk(GGate(nc, vregs, "vid", 2), bufs, y_sb, p_sb,
                              zacc, 3072, 512, False)
            e2 = sm.snap()
        ends = [e0, e1, e2, s4]

        for j in range(4):
            sm.restore(ends[j])
            gj = GGate(nc, vregs, "vid", j)
            v = VM[j]
            O, P = v["O"], v["P"]
            pfx = f"a{j}_"
            with ExitStack() as esO:
                sbo = lambda n, sh, dt: esO.enter_context(
                    nc.sbuf_tensor(pfx + n, sh, dt))
                att = {
                    "KVB": KVB,
                    "kt": sbo("kt", [128, P], BF16),
                    "vk": sbo("vk", [128, P], BF16),
                    "q_sb": sbo("q", [128, NH * O], BF16),
                    "wqs": [sbo(f"wqs{i}", [128, 256], F32) for i in range(3)],
                    "wqb": [sbo(f"wqb{i}", [128, 256], BF16) for i in range(2)],
                    "xsw": [sbo(f"xsw{i}", [128, 1024], F32) for i in range(2)],
                    "r0": sbo("r0", [128, 512], F32),
                    "mk4": sbo("mk4", [128, 2048], BF16),
                    "idnb": sbo("idnb", [128, 128], BF16),
                    "idnf": sbo("idnf", [128, 128], F32),
                    "onesb": onesb, "onesfr": onesfr,
                    "otr": sbo("otr", [1, NH * NSC_MAX], F32),
                    "sums": sbo("sums", [128, 2 * NSC_MAX], F32),
                    "ssum": sbo("ssum", [128, 1], F32),
                    "srow": sbo("srow", [1, 128], F32),
                    "rb": sbo("rb", [128, 128], F32),
                    "bsc": sbo("bsc", [1, 96], F32),
                    "bcol": sbo("bcol", [128, 32], F32),
                    "ba": sbo("ba", [128, NH * NSC_MAX], F32),
                }
                b_cl = sm.n["qwld0"]
                with gj.Block() as blk:
                    @blk.sync
                    def _(s):
                        for dst, src in ((att["r0"], R0), (att["mk4"], MK4),
                                         (att["idnb"], IDB), (att["idnf"], IDF),
                                         (att["otr"], OTR)):
                            sm.inc(s.dma_start(out=dst[:, :], in_=src[:, :]),
                                   "qwld0", 16)
                        s.wait_ge(sm["qwld0"], b_cl + 80)
                sm.n["qwld0"] = b_cl + 80
                with ExitStack() as esA:
                    att["vt"] = esA.enter_context(
                        nc.sbuf_tensor(pfx + "vt", [128, P], BF16))
                    att["y2s"] = esA.enter_context(
                        nc.sbuf_tensor(pfx + "y2s", [128, NPT * 1024], BF16))
                    _emit_vkeys(nc, gj, sm, att, P)
                    for (so, T) in _chunks_of(BKS[j], BKS[j + 1]):
                        _emit_qproj(nc, gj, sm, att, WQP, YB, T, so,
                                    so - BKS[j], O)
                with ExitStack() as esB:
                    sbb = lambda n, sh, dt: esB.enter_context(
                        nc.sbuf_tensor(pfx + n, sh, dt))
                    att["qsq"] = sbb("qsq", [128, max(O, 1024)], BF16)
                    att["nrow"] = sbb("nrow", [1, 1536], F32)
                    att["sc2"] = [sbb(f"sc2{i}", [128, 512], F32)
                                  for i in range(2)]
                    att["pr"] = [sbb(f"pr{i}", [128, 512], BF16)
                                 for i in range(4)]
                    att["pt"] = sbb("pt", [128, 512 * max(v["ext"])], BF16)
                    att["relh"] = sbb("relh", [128, 512], F32)
                    att["rm4"] = sbb("rm4", [128, 2048], F32)
                    _emit_kmax(nc, gj, sm, att, P)
                    _emit_norms(nc, gj, sm, att, j)
                    _emit_einsum(nc, gj, sm, att, j)
                for (so, T) in _chunks_of(BKS[j], BKS[j + 1]):
                    _emit_wo(nc, gj, sm, att, WOP, XB, XB, T, so,
                             so - BKS[j], O)
            with ExitStack() as esF:
                bufs2 = _common_bufs(nc, esF, f"f{j}_")
                fill_common(bufs2)
                y2 = esF.enter_context(
                    nc.sbuf_tensor(f"f{j}_y", [128, NPT * 1024], BF16))
                p2 = esF.enter_context(
                    nc.sbuf_tensor(f"f{j}_p", [128, 8192], BF16))
                z2 = esF.enter_context(
                    nc.sbuf_tensor(f"f{j}_z", [128, 8192], F32))
                for (so, T) in _chunks_of(BKS[j], BKS[j + 1]):
                    pre = [("wst20", sm.n["wst20"]), ("wst21", sm.n["wst21"]),
                           ("fst0", sm.n["fst0"]), ("fst1", sm.n["fst1"])]
                    _emit_ln(nc, gj, sm, bufs2, XB, y2, T, so, pre=pre)
                    _emit_ffn(nc, gj, sm, bufs2, y2, p2, z2, XB, OUT, wd, T,
                              so, so - BKS[j], True)
    return nc



def kernel(**inputs):
    x = np.asarray(inputs["L0_attn_hidden_states"], np.float32)
    wp = _pack_weights(
        np.asarray(inputs["L0_attn_Wq"], np.float32),
        np.asarray(inputs["L0_attn_Wkv"], np.float32),
        np.asarray(inputs["L0_attn_Wo"], np.float32),
        np.asarray(inputs["L0_ffn_U1"], np.float32),
        np.asarray(inputs["L0_ffn_V1"], np.float32),
        np.asarray(inputs["L0_ffn_U2"], np.float32),
        np.asarray(inputs["L0_ffn_V2"], np.float32),
        np.asarray(inputs["L0_ffn_b1"], np.float32),
        np.asarray(inputs["L0_ffn_b2"], np.float32))
    cs = _consts_shared()
    nc = build_fused()
    ins = []
    for c in range(8):
        b, j = c // 4, c % 4
        xT = np.ascontiguousarray(x[b].T.reshape(NPT, 128, S))
        vid = np.array([[j, 1 if j >= 1 else 0, 1 if j >= 2 else 0, 0]],
                       np.int32)
        d = {"x": xT, "vid": vid, "otr": _otab_row(j),
             "u1p": wp["u1p"], "v1u2p": wp["v1u2p"], "v2p": wp["v2p"],
             "wqp": wp["wqp"], "wop": wp["wop"], "wkvp": wp["wkvp"],
             "b1p": wp["b1p"], "b2p": wp["b2p"],
             "r0": cs["r0"], "mask4": cs["mask4"],
             "idn_bf": cs["idn_bf"], "idn_f32": cs["idn_f32"],
             "ones_bf": cs["ones_bf"], "ones_f32r": cs["ones_f32r"]}
        ins.append(d)
    res = run_bass_kernel_spmd(nc, ins, core_ids=list(range(8)))
    out = np.empty((NB, S, D), np.float32)
    for c in range(8):
        b, j = c // 4, c % 4
        O = BKS[j + 1] - BKS[j]
        a = np.asarray(res.results[c]["out"])
        out[b, BKS[j]:BKS[j + 1], :] = a[:, :, 0:O].reshape(D, O).T
    return out


# version 07 (final)
# speedup vs baseline: 1.1184x; 1.1184x over previous
import math
from contextlib import ExitStack

import numpy as np
import ml_dtypes

import concourse.bass as bass
import concourse.mybir as mybir
from concourse.bass_utils import run_bass_kernel_spmd

F32 = mybir.dt.float32
BF16 = mybir.dt.bfloat16
I32 = mybir.dt.int32
AF = mybir.ActivationFunctionType
ALU = mybir.AluOpType
AX = mybir.AxisListType.X
bfdt = ml_dtypes.bfloat16

NB, S, D = 2, 4096, 4096
NH, HD = 32, 128
INTER, RANK = 11008, 1024
NPT = 32
NI = 86
SCALE = 1.0 / math.sqrt(HD)
EPS = 1e-5
NEG = -1e30
SLOPES = [2.0 ** (-(i + 1) / 4.0) for i in range(NH)]

BKS = [0, 1536, 2560, 3584, 4096]
OMAX = 1536


def _chunks_of(n0, n1):
    out = []
    o = n0
    while n1 - o >= 1024:
        out.append((o, 1024))
        o += 1024
    if n1 - o:
        out.append((o, n1 - o))
        o = n1
    return out



def _pack_weights(Wq, Wkv, Wo, U1, V1, U2, V2, b1, b2):
    p = {}
    p["u1p"] = np.ascontiguousarray(
        U1.reshape(32, 128, 8, 128).transpose(2, 1, 0, 3).reshape(8, 128, 4096))
    v1t = V1.reshape(8, 128, 172, 128).transpose(2, 1, 0, 3).reshape(172, 128, 1024)
    u2t = U2.reshape(86, 128, 8, 128).reshape(86, 128, 1024)
    p["v1u2p"] = np.ascontiguousarray(
        np.concatenate([v1t[:86], v1t[86:], u2t], axis=2))
    p["v2p"] = np.ascontiguousarray(
        V2.reshape(8, 128, 32, 128).transpose(2, 1, 0, 3).reshape(32, 128, 1024))
    p["wqp"] = np.ascontiguousarray(
        Wq.reshape(32, 128, 16, 256).transpose(2, 1, 0, 3).reshape(16, 128, 8192))
    p["wop"] = np.ascontiguousarray(
        Wo.reshape(32, 128, 16, 256).transpose(2, 1, 0, 3).reshape(16, 128, 8192))
    p["wkvp"] = np.ascontiguousarray(
        Wkv.reshape(32, 128, 256).transpose(1, 0, 2).reshape(128, 8192))
    p["b1p"] = np.ascontiguousarray(b1.reshape(172, 128).T)
    p["b2p"] = np.ascontiguousarray(b2.reshape(32, 128).T)
    return p


def _variant_meta(j):
    t0, t1 = BKS[j] // 128, BKS[j + 1] // 128
    tiles = list(range(t0, t1))
    if j == 0:
        tiles = [0, 4, 1, 5, 2, 6, 3, 7, 8, 9, 10, 11]
    ext = [t // 4 + 1 for t in tiles]
    nsc = sum(ext)
    scidx = {}
    k = 0
    for s, e in enumerate(ext):
        for c in range(e):
            scidx[(s, c)] = k
            k += 1
    return dict(tiles=tiles, ext=ext, nsc=nsc, scidx=scidx, t0=t0,
                O=BKS[j + 1] - BKS[j], P=BKS[j + 1])


VM = [_variant_meta(j) for j in range(4)]
NSC_MAX = max(v["nsc"] for v in VM)


def _consts_shared():
    c = {}
    dq = np.arange(128, dtype=np.float32)[:, None]
    dk = np.arange(512, dtype=np.float32)[None, :]
    c["r0"] = np.ascontiguousarray((dk - dq).astype(np.float32))
    m4 = [np.where(dk - dq <= 128.0 * m, 0.0, NEG).astype(np.float32)
          for m in range(4)]
    c["mask4"] = np.ascontiguousarray(np.concatenate(m4, axis=1).astype(bfdt))
    c["idn_bf"] = np.eye(128, dtype=bfdt)
    c["idn_f32"] = np.eye(128, dtype=np.float32)
    c["ones_bf"] = np.ones((128, 1), dtype=bfdt)
    c["ones_f32r"] = np.ones((1, 128), dtype=np.float32)
    return c


def _otab_row(j):
    v = VM[j]
    row = np.zeros((1, NH * NSC_MAX), np.float32)
    for h in range(NH):
        for s, t in enumerate(v["tiles"]):
            for cc in range(v["ext"][s]):
                row[0, h * NSC_MAX + v["scidx"][(s, cc)]] = \
                    SLOPES[h] * (512.0 * cc - 128.0 * t)
    return row



class Sems:
    def __init__(self, nc, es, names):
        self.h = {n: es.enter_context(nc.semaphore(n)) for n in names}
        self.n = {n: 0 for n in names}

    def snap(self):
        return dict(self.n)

    def restore(self, st):
        self.n = dict(st)

    def inc(self, instr, name, k=1):
        instr.then_inc(self.h[name], k)
        self.n[name] += k
        return self.n[name]

    def __getitem__(self, name):
        return self.h[name]


class VPad:

    def __init__(self, eng):
        self._e = eng

    def __getattr__(self, name):
        a = getattr(self._e, name)
        if callable(a):
            def wrap(*args, **kw):
                r = a(*args, **kw)
                self._e.nop()
                return r
            return wrap
        return a


class GBlock:

    def __init__(self, blk, vregs, key, case):
        self.blk, self.vregs, self.key, self.case = blk, vregs, key, case

    def _mk(self, name):
        def deco(f):
            if self.key is None:
                body = f
            else:
                def body(e):
                    with e.If_eq(self.vregs[name][self.key], self.case):
                        f(VPad(e) if name == "vector" else e)
            getattr(self.blk, name)(body)
        return deco

    @property
    def gpsimd(self):
        return self._mk("gpsimd")

    @property
    def sync(self):
        return self._mk("sync")

    @property
    def scalar(self):
        return self._mk("scalar")

    @property
    def vector(self):
        return self._mk("vector")

    @property
    def tensor(self):
        return self._mk("tensor")


class GGate:

    def __init__(self, nc, vregs, key, case):
        self.nc, self.vregs, self.key, self.case = nc, vregs, key, case

    def Block(self):
        class _Ctx:
            def __init__(s2):
                s2.cm = self.nc.Block()

            def __enter__(s2):
                blk = s2.cm.__enter__()
                return GBlock(blk, self.vregs, self.key, self.case)

            def __exit__(s2, *a):
                return s2.cm.__exit__(*a)
        return _Ctx()


def _common_bufs(nc, es, pfx=""):
    sb = lambda n, sh, dt: es.enter_context(nc.sbuf_tensor(pfx + n, sh, dt))
    bufs = {
        "xst": [sb(f"xst{i}", [128, 1024], F32) for i in range(2)],
        "xbf": [sb(f"xbf{i}", [128, 1024], BF16) for i in range(2)],
        "xsq": [sb(f"xsq{i}", [128, 1024], BF16) for i in range(2)],
        "row": sb("row", [1, 3072], F32),
        "mb_sb": sb("mb", [128, 1024], F32),
        "ib_sb": sb("ib", [128, 1024], F32),
        "u1st": [sb(f"u1st{i}", [128, 512], F32) for i in range(2)],
        "u1bf": [sb(f"u1bf{i}", [128, 512], BF16) for i in range(2)],
        "wst": [sb(f"wst{i}", [128, 1536], F32) for i in range(2)],
        "wbf": [sb(f"wbf{i}", [128, 3072], BF16) for i in range(2)],
        "v2st": [sb(f"v2st{i}", [128, 512], F32) for i in range(2)],
        "v2bf": [sb(f"v2bf{i}", [128, 1024], BF16) for i in range(2)],
        "sg_sb": sb("sg", [128, 1024], F32),
        "ub_sb": sb("ub", [128, 1024], F32),
        "act_bf": [sb(f"act{i}", [128, 1024], BF16) for i in range(2)],
        "b1p": None, "b2h": None, "ones_bf": None, "ones_f32r": None,
    }
    bufs["xst2"] = bufs["xst"]
    bufs["t_sb"] = [bufs["sg_sb"], bufs["ub_sb"]]
    return bufs


LN_SEMS = ["lnld0", "lnld1", "lncast", "lnmm", "lnfin", "lnap",
           "lnld20", "lnld21"]
FFN_SEMS = ["fu1ld0", "fu1ld1", "fu1c", "fu1u", "fpev",
            "fwld0", "fwld1", "fwc",
            "fhmm", "fhdr", "fact", "fzmm", "fzd",
            "fv2ld0", "fv2ld1", "fv2c",
            "fomm", "ft", "fxld0", "fxld1", "fxadd", "fst0", "fst1", "fzcast"]
KV_SEMS = ["kvld0", "kvld1", "kvc", "kvmm", "kvev", "yst"]
ATT_SEMS = ["vktr", "vkev", "knsq", "knmm", "knev",
            "qwld0", "qwld1", "qwc", "qwu", "qev", "qyld0", "qyld1",
            "qnsq", "qnmm", "qnev", "bch",
            "scmm", "scv", "sexp", "strc", "ptev", "spv",
            "nsum", "nrow1", "nrec", "nrb2", "nrbc", "ndone", "relw",
            "bldr", "bldv",
            "wwld0", "wwld1", "wwc", "wwu", "wxld0", "wxld1", "wxadd",
            "wst20", "wst21"]


def _emit_ln(nc, gb, sm, bufs, src_dram, y_sb, T, so, pre=None):
    H = T // 512
    with ExitStack() as pse:
        psf = lambda n, sh: pse.enter_context(nc.psum_tensor(n + _uid(), sh, F32))
        sx = [psf(f"lnsx{i}", [1, 512]) for i in range(H)]
        sq = [psf(f"lnsq{i}", [1, 512]) for i in range(H)]
        bc = [psf(f"lnbc{i}", [128, 512]) for i in range(2 * H)]
        xst, xbf, xsq = bufs["xst"], bufs["xbf"], bufs["xsq"]
        row, mb_sb, ib_sb = bufs["row"], bufs["mb_sb"], bufs["ib_sb"]
        ones_bf, ones_f32r = bufs["ones_bf"], bufs["ones_f32r"]
        b_ld = [sm.n["lnld0"], sm.n["lnld1"]]
        b_ld2 = [sm.n["lnld20"], sm.n["lnld21"]]
        b_cast = sm.n["lncast"]; b_mm = sm.n["lnmm"]
        b_fin = sm.n["lnfin"]; b_ap = sm.n["lnap"]

        with gb.Block() as blk:
            @blk.gpsimd
            def _(g):
                for pt in range(NPT):
                    if pt == 0 and pre is not None:
                        for pn, pv in pre:
                            g.wait_ge(sm[pn], pv)
                    if pt >= 2:
                        g.wait_ge(sm["lncast"], b_cast + pt - 1)
                    sm.inc(g.dma_start(out=xst[pt % 2][:, 0:T],
                                       in_=src_dram[pt, :, so:so + T]),
                           f"lnld{pt % 2}", 16)

            @blk.scalar
            def _(s):
                for pt in range(NPT):
                    s.wait_ge(sm[f"lnld{pt % 2}"], b_ld[pt % 2] + 16 * (pt // 2 + 1))
                    if pt >= 2:
                        s.wait_ge(sm["lnmm"], b_mm + pt - 1)
                    s.activation(xbf[pt % 2][:, 0:T], xst[pt % 2][:, 0:T], AF.Copy)
                    sm.inc(s.activation(xsq[pt % 2][:, 0:T], xst[pt % 2][:, 0:T],
                                        AF.Square), "lncast")
                s.wait_ge(sm["lnmm"], b_mm + NPT)
                for h2 in range(H):
                    last = s.activation(row[:, h2 * 512:(h2 + 1) * 512], sx[h2][:, :],
                                        AF.Copy, scale=1.0 / 4096.0)
                sm.inc(last, "lnfin")
                s.wait_ge(sm["lnfin"], b_fin + 4)
                sm.inc(s.activation(row[:, 2048:2048 + T], row[:, 2048:2048 + T],
                                    AF.Sqrt), "lnfin")
                s.wait_ge(sm["lnmm"], b_mm + NPT + 1)
                for h2 in range(H):
                    s.activation(mb_sb[:, h2 * 512:(h2 + 1) * 512], bc[h2][:, :],
                                 AF.Copy)
                    last = s.activation(ib_sb[:, h2 * 512:(h2 + 1) * 512],
                                        bc[H + h2][:, :], AF.Copy)
                sm.inc(last, "lnfin")
                for pt in range(NPT):
                    if pt < 2:
                        s.wait_ge(sm["lncast"], b_cast + 31 + pt)
                    else:
                        s.wait_ge(sm["lnap"], b_ap + 2 * (pt - 1))
                    sm.inc(s.dma_start(out=xst[pt % 2][:, 0:T],
                                       in_=src_dram[pt, :, so:so + T]),
                           f"lnld2{pt % 2}", 16)

            @blk.tensor
            def _(t):
                for pt in range(NPT):
                    t.wait_ge(sm["lncast"], b_cast + pt + 1)
                    for h2 in range(H):
                        c0 = h2 * 512
                        t.matmul(sx[h2][:, :], ones_bf[:, :],
                                 xbf[pt % 2][:, c0:c0 + 512],
                                 start=(pt == 0), stop=(pt == NPT - 1))
                        last = t.matmul(sq[h2][:, :], ones_bf[:, :],
                                        xsq[pt % 2][:, c0:c0 + 512],
                                        start=(pt == 0), stop=(pt == NPT - 1))
                    sm.inc(last, "lnmm")
                t.wait_ge(sm["lnfin"], b_fin + 6)
                for h2 in range(H):
                    t.matmul(bc[h2][:, :], ones_f32r[:, :],
                             row[:, h2 * 512:(h2 + 1) * 512], start=True, stop=True)
                    last = t.matmul(bc[H + h2][:, :], ones_f32r[:, :],
                                    row[:, 1024 + h2 * 512:1024 + (h2 + 1) * 512],
                                    start=True, stop=True)
                sm.inc(last, "lnmm")

            @blk.vector
            def _(v):
                v.wait_ge(sm["lnfin"], b_fin + 1)
                sm.inc(v.tensor_mul(row[:, 1024:1024 + T], row[:, 0:T], row[:, 0:T]),
                       "lnfin")
                v.wait_ge(sm["lnfin"], b_fin + 2)
                for h2 in range(H):
                    last = v.scalar_tensor_tensor(
                        row[:, 2048 + h2 * 512:2048 + (h2 + 1) * 512], sq[h2][:, :],
                        1.0 / 4096.0, row[:, 1024 + h2 * 512:1024 + (h2 + 1) * 512],
                        ALU.mult, ALU.subtract)
                sm.inc(last, "lnfin")
                v.wait_ge(sm["lnfin"], b_fin + 3)
                sm.inc(v.tensor_scalar(row[:, 2048:2048 + T], row[:, 2048:2048 + T],
                                       EPS, None, ALU.add), "lnfin")
                v.wait_ge(sm["lnfin"], b_fin + 5)
                sm.inc(v.reciprocal(row[:, 1024:1024 + T], row[:, 2048:2048 + T]),
                       "lnfin")
                v.wait_ge(sm["lnfin"], b_fin + 7)
                for pt in range(NPT):
                    v.wait_ge(sm[f"lnld2{pt % 2}"], b_ld2[pt % 2] + 16 * (pt // 2 + 1))
                    sm.inc(v.tensor_sub(xst[pt % 2][:, 0:T], xst[pt % 2][:, 0:T],
                                        mb_sb[:, 0:T]), "lnap")
                    v.wait_ge(sm["lnap"], b_ap + 2 * pt + 1)
                    sm.inc(v.tensor_mul(y_sb[:, pt * 1024:pt * 1024 + T],
                                        xst[pt % 2][:, 0:T], ib_sb[:, 0:T]), "lnap")


_UID = [0]


def _uid():
    _UID[0] += 1
    return f"_{_UID[0]}"


def _emit_ffn(nc, gb, sm, bufs, y_sb, p_sb, zacc, xsrc, xdst, wd, T, so, do,
              is_final):
    H = T // 512
    with ExitStack() as pse:
        psf = lambda n, sh: pse.enter_context(nc.psum_tensor(n + _uid(), sh, F32))
        q4 = [psf(f"fq{i}", [128, 512]) for i in range(2 * H)]
        zp = [psf(f"fzp{i}", [128, 1024]) for i in range(2)]
        u1st, u1bf = bufs["u1st"], bufs["u1bf"]
        wst, wbf = bufs["wst"], bufs["wbf"]
        v2st, v2bf = bufs["v2st"], bufs["v2bf"]
        sg_sb, ub_sb = bufs["sg_sb"], bufs["ub_sb"]
        act_bf, t_sb, xst2 = bufs["act_bf"], bufs["t_sb"], bufs["xst2"]
        b1p, b2h = bufs["b1p"], bufs["b2h"]
        b = {k: sm.n[k] for k in FFN_SEMS}

        def q4i(par, h2):
            return q4[H * par + h2] if H == 2 else q4[par]

        with gb.Block() as blk:
            @blk.sync
            def _(s):
                for pm in range(8):
                    for kc in range(8):
                        n = pm * 8 + kc
                        if n >= 2:
                            s.wait_ge(sm["fu1c"], b["fu1c"] + n - 1)
                        sm.inc(s.dma_start(out=u1st[n % 2][:, :],
                                           in_=wd["u1p"][pm, :, kc * 512:(kc + 1) * 512]),
                               f"fu1ld{n % 2}", 16)
                for i in range(NI):
                    for hf in range(2):
                        n = 2 * i + hf
                        if n >= 2:
                            s.wait_ge(sm["fwc"], b["fwc"] + n - 1)
                        sm.inc(s.dma_start(
                            out=wst[n % 2][:, :],
                            in_=wd["v1u2p"][i, :, hf * 1536:(hf + 1) * 1536]),
                               f"fwld{n % 2}", 16)
                for om in range(34):
                    if om < 32:
                        for hf in range(2):
                            n = 2 * om + hf
                            if n >= 2:
                                s.wait_ge(sm["fv2c"], b["fv2c"] + n - 1)
                            sm.inc(s.dma_start(
                                out=v2st[n % 2][:, :],
                                in_=wd["v2p"][om, :, hf * 512:(hf + 1) * 512]),
                                   f"fv2ld{n % 2}", 16)
                    ol = om - 1
                    if 0 <= ol < 32:
                        if ol >= 2:
                            s.wait_ge(sm[f"fst{ol % 2}"],
                                      b[f"fst{ol % 2}"] + 16 * (ol // 2))
                        sm.inc(s.dma_start(out=xst2[ol % 2][:, 0:T],
                                           in_=xsrc[ol, :, so:so + T]),
                               f"fxld{ol % 2}", 16)

            @blk.gpsimd
            def _(g):
                for os_ in range(32):
                    g.wait_ge(sm["fxadd"], b["fxadd"] + os_ + 1)
                    sm.inc(g.dma_start(out=xdst[os_, :, do:do + T],
                                       in_=xst2[os_ % 2][:, 0:T]),
                           f"fst{os_ % 2}", 16)
                if is_final:
                    g.wait_ge(sm["fst0"], b["fst0"] + 16 * 16)
                    g.wait_ge(sm["fst1"], b["fst1"] + 16 * 16)

            @blk.scalar
            def _(s):
                def u1cast(n):
                    s.wait_ge(sm[f"fu1ld{n % 2}"],
                              b[f"fu1ld{n % 2}"] + 16 * (n // 2 + 1))
                    if n >= 2:
                        s.wait_ge(sm["fu1u"], b["fu1u"] + n - 1)
                    sm.inc(s.activation(u1bf[n % 2][:, :], u1st[n % 2][:, :],
                                        AF.Copy), "fu1c")

                u1cast(0)
                u1cast(1)
                for pm in range(8):
                    for kc in range(8):
                        n = pm * 8 + kc
                        if n + 2 < 64:
                            u1cast(n + 2)
                    if pm >= 1:
                        s.wait_ge(sm["fu1u"], b["fu1u"] + 8 * pm)
                        for h2 in range(H):
                            c0 = (pm - 1) * 1024 + h2 * 512
                            last = s.activation(p_sb[:, c0:c0 + 512],
                                                q4i((pm - 1) % 2, h2)[:, :], AF.Copy)
                        sm.inc(last, "fpev")
                s.wait_ge(sm["fu1u"], b["fu1u"] + 64)
                for h2 in range(H):
                    c0 = 7 * 1024 + h2 * 512
                    last = s.activation(p_sb[:, c0:c0 + 512], q4i(1, h2)[:, :],
                                        AF.Copy)
                sm.inc(last, "fpev")

                def wcast(n):
                    i2, hf = n // 2, n % 2
                    s.wait_ge(sm[f"fwld{n % 2}"],
                              b[f"fwld{n % 2}"] + 16 * (n // 2 + 1))
                    if i2 >= 2 and hf == 0:
                        s.wait_ge(sm["fzmm"], b["fzmm"] + 8 * (i2 - 1))
                    sm.inc(s.activation(wbf[i2 % 2][:, hf * 1536:(hf + 1) * 1536],
                                        wst[n % 2][:, :], AF.Copy), "fwc")

                for n in range(4):
                    wcast(n)
                for i in range(NI):
                    s.wait_ge(sm["fhmm"], b["fhmm"] + i + 1)
                    if i >= 1:
                        s.wait_ge(sm["fact"], b["fact"] + i)
                    for h2 in range(H):
                        c0 = h2 * 512
                        s.activation(sg_sb[:, c0:c0 + 512], q4i(0, h2)[:, :], AF.Silu,
                                     bias=b1p[:, i:i + 1])
                        last = s.activation(ub_sb[:, c0:c0 + 512], q4i(1, h2)[:, :],
                                            AF.Identity, bias=b1p[:, 86 + i:87 + i])
                    sm.inc(last, "fhdr")
                    for hf in range(2):
                        n = 2 * (i + 2) + hf
                        if n < 2 * NI:
                            wcast(n)
                s.wait_ge(sm["fzd"], b["fzd"] + NI * 8)
                if T == 1024:
                    sm.inc(s.activation(p_sb[:, :], zacc[:, :], AF.Copy),
                           "fzcast")
                else:
                    for zm in range(8):
                        last = s.activation(p_sb[:, zm * 1024:zm * 1024 + T],
                                            zacc[:, zm * 1024:zm * 1024 + T],
                                            AF.Copy)
                    sm.inc(last, "fzcast")

                def v2cast(n):
                    om2, hf = n // 2, n % 2
                    s.wait_ge(sm[f"fv2ld{n % 2}"],
                              b[f"fv2ld{n % 2}"] + 16 * (n // 2 + 1))
                    if om2 >= 2 and hf == 0:
                        s.wait_ge(sm["fomm"], b["fomm"] + om2 - 1)
                    sm.inc(s.activation(v2bf[om2 % 2][:, hf * 512:(hf + 1) * 512],
                                        v2st[n % 2][:, :], AF.Copy), "fv2c")

                for n in range(4):
                    v2cast(n)
                for om in range(32):
                    s.wait_ge(sm["fomm"], b["fomm"] + om + 1)
                    if om >= 2:
                        s.wait_ge(sm["fxadd"], b["fxadd"] + om - 1)
                    for h2 in range(H):
                        last = s.activation(t_sb[om % 2][:, h2 * 512:(h2 + 1) * 512],
                                            q4i(om % 2, h2)[:, :], AF.Identity,
                                            scale=0.5, bias=b2h[:, om:om + 1])
                    sm.inc(last, "ft")
                    for hf in range(2):
                        n = 2 * (om + 2) + hf
                        if n < 64:
                            v2cast(n)

            @blk.tensor
            def _(t):
                for pm in range(8):
                    if pm >= 2:
                        t.wait_ge(sm["fpev"], b["fpev"] + pm - 1)
                    for kc in range(8):
                        n = pm * 8 + kc
                        t.wait_ge(sm["fu1c"], b["fu1c"] + n + 1)
                        for jj in range(4):
                            k = kc * 4 + jj
                            for h2 in range(H):
                                last = t.matmul(q4i(pm % 2, h2)[:, :],
                                                u1bf[n % 2][:, jj * 128:(jj + 1) * 128],
                                                y_sb[:, k * 1024 + h2 * 512:k * 1024 + h2 * 512 + 512],
                                                start=(k == 0), stop=(k == 31))
                        sm.inc(last, "fu1u")
                for i in range(NI + 1):
                    if i < NI:
                        t.wait_ge(sm["fwc"], b["fwc"] + 2 * (i + 1))
                        t.wait_ge(sm["fhdr"], b["fhdr"] + i)
                        if i == 0:
                            t.wait_ge(sm["fpev"], b["fpev"] + 8)
                        for part in range(2):
                            for k in range(8):
                                for h2 in range(H):
                                    cw = part * 1024 + k * 128
                                    last = t.matmul(
                                        q4i(part, h2)[:, :],
                                        wbf[i % 2][:, cw:cw + 128],
                                        p_sb[:, k * 1024 + h2 * 512:k * 1024 + h2 * 512 + 512],
                                        start=(k == 0), stop=(k == 7))
                        sm.inc(last, "fhmm")
                    if i >= 1:
                        ii = i - 1
                        t.wait_ge(sm["fact"], b["fact"] + ii + 1)
                        for zm in range(8):
                            g = ii * 8 + zm
                            if g >= 2:
                                t.wait_ge(sm["fzd"], b["fzd"] + g - 1)
                            for h2 in range(H):
                                last = t.matmul(
                                    zp[g % 2][:, h2 * 512:(h2 + 1) * 512],
                                    wbf[ii % 2][:, 2048 + zm * 128:2048 + (zm + 1) * 128],
                                    act_bf[ii % 2][:, h2 * 512:(h2 + 1) * 512],
                                    start=True, stop=True)
                            sm.inc(last, "fzmm")
                for om in range(32):
                    t.wait_ge(sm["fv2c"], b["fv2c"] + 2 * (om + 1))
                    if om == 0:
                        t.wait_ge(sm["fzcast"], b["fzcast"] + 1)
                        t.wait_ge(sm["fhdr"], b["fhdr"] + NI)
                    if om >= 2:
                        t.wait_ge(sm["ft"], b["ft"] + om - 1)
                    for k in range(8):
                        for h2 in range(H):
                            last = t.matmul(q4i(om % 2, h2)[:, :],
                                            v2bf[om % 2][:, k * 128:(k + 1) * 128],
                                            p_sb[:, k * 1024 + h2 * 512:k * 1024 + h2 * 512 + 512],
                                            start=(k == 0), stop=(k == 7))
                    sm.inc(last, "fomm")

            @blk.vector
            def _(v):
                for i in range(NI):
                    if i >= 1:
                        ii = i - 1
                        for zm in range(8):
                            g = ii * 8 + zm
                            v.wait_ge(sm["fzmm"], b["fzmm"] + g + 1)
                            dst = zacc[:, zm * 1024:zm * 1024 + T]
                            if ii == 0:
                                sm.inc(v.tensor_copy(dst, zp[g % 2][:, 0:T]), "fzd")
                            else:
                                sm.inc(v.tensor_add(dst, dst, zp[g % 2][:, 0:T]),
                                       "fzd")
                    v.wait_ge(sm["fhdr"], b["fhdr"] + i + 1)
                    sm.inc(v.tensor_mul(act_bf[i % 2][:, 0:T], sg_sb[:, 0:T],
                                        ub_sb[:, 0:T]), "fact")
                ii = NI - 1
                for zm in range(8):
                    g = ii * 8 + zm
                    v.wait_ge(sm["fzmm"], b["fzmm"] + g + 1)
                    dst = zacc[:, zm * 1024:zm * 1024 + T]
                    sm.inc(v.tensor_add(dst, dst, zp[g % 2][:, 0:T]), "fzd")
                for om in range(32):
                    v.wait_ge(sm["ft"], b["ft"] + om + 1)
                    v.wait_ge(sm[f"fxld{om % 2}"],
                              b[f"fxld{om % 2}"] + 16 * (om // 2 + 1))
                    sm.inc(v.tensor_add(xst2[om % 2][:, 0:T], xst2[om % 2][:, 0:T],
                                        t_sb[om % 2][:, 0:T]), "fxadd")


def _emit_kv(nc, gb, sm, bufs, y_sb, WKVP, KVB, YB, T, so):
    H = T // 512
    with ExitStack() as pse:
        kvp = [pse.enter_context(nc.psum_tensor(f"kvp{i}" + _uid(), [128, 512], F32))
               for i in range(2 * H)]
        kvw, kvwb = bufs["u1st"], bufs["u1bf"]
        kv_bf = bufs["act_bf"]
        b = {k: sm.n[k] for k in KV_SEMS + ["lnap"]}

        def kvpi(m2, h2):
            return kvp[H * m2 + h2] if H == 2 else kvp[m2]

        with gb.Block() as blk:
            @blk.gpsimd
            def _(g):
                for pt in range(NPT):
                    g.wait_ge(sm["lnap"], b["lnap"])
                    sm.inc(g.dma_start(out=YB[pt, :, so:so + T],
                                       in_=y_sb[:, pt * 1024:pt * 1024 + T]),
                           "yst", 16)
                g.wait_ge(sm["kvev"], b["kvev"] + 1)
                for m2 in range(2):
                    sm.inc(g.dma_start(out=KVB[m2, :, so:so + T],
                                       in_=kv_bf[m2][:, 0:T]), "yst", 16)
                g.wait_ge(sm["yst"], b["yst"] + 16 * (NPT + 2))

            @blk.sync
            def _(s):
                for k in range(32):
                    if k >= 2:
                        s.wait_ge(sm["kvc"], b["kvc"] + k - 1)
                    sm.inc(s.dma_start(out=kvw[k % 2][:, 0:256],
                                       in_=WKVP[:, k * 256:(k + 1) * 256]),
                           f"kvld{k % 2}", 16)

            @blk.scalar
            def _(s):
                for k in range(32):
                    s.wait_ge(sm[f"kvld{k % 2}"],
                              b[f"kvld{k % 2}"] + 16 * (k // 2 + 1))
                    if k >= 2:
                        s.wait_ge(sm["kvmm"], b["kvmm"] + k - 1)
                    sm.inc(s.activation(kvwb[k % 2][:, 0:256], kvw[k % 2][:, 0:256],
                                        AF.Copy), "kvc")
                s.wait_ge(sm["kvmm"], b["kvmm"] + 32)
                for m2 in range(2):
                    for h2 in range(H):
                        last = s.activation(kv_bf[m2][:, h2 * 512:(h2 + 1) * 512],
                                            kvpi(m2, h2)[:, :], AF.Copy)
                sm.inc(last, "kvev")

            @blk.tensor
            def _(t):
                for k in range(32):
                    t.wait_ge(sm["kvc"], b["kvc"] + k + 1)
                    for m2 in range(2):
                        for h2 in range(H):
                            last = t.matmul(
                                kvpi(m2, h2)[:, :],
                                kvwb[k % 2][:, m2 * 128:(m2 + 1) * 128],
                                y_sb[:, k * 1024 + h2 * 512:k * 1024 + h2 * 512 + 512],
                                start=(k == 0), stop=(k == 31))
                    sm.inc(last, "kvmm")


def _emit_qproj(nc, gb, sm, att, WQP, YB, so, OC):
    nh2 = OC // 512
    par = 2 if nh2 <= 2 else 1
    y2s, q_sb, wqs, wqb = att["y2s"], att["q_sb"], att["wqs"], att["wqb"]
    with ExitStack() as pse:
        qp = [pse.enter_context(nc.psum_tensor(f"qp{i}" + _uid(), [128, 512], F32))
              for i in range(par * 2 * nh2)]
        b = {k: sm.n[k] for k in ("qwld0", "qwld1", "qwc", "qwu", "qev",
                                  "qyld0", "qyld1")}

        def qpi(p, m2, h2):
            return qp[(2 * nh2) * p + nh2 * m2 + h2]

        with gb.Block() as blk:
            @blk.gpsimd
            def _(g):
                for pt in range(NPT):
                    sm.inc(g.dma_start(out=y2s[:, pt * OC:(pt + 1) * OC],
                                       in_=YB[pt, :, so:so + OC]),
                           f"qyld{pt % 2}", 16)

            @blk.sync
            def _(s):
                for n in range(0, 512, 2):
                    if n >= 2:
                        s.wait_ge(sm["qwc"], b["qwc"] + n - 1)
                    sm.inc(s.dma_start(
                        out=wqs[0][:, :],
                        in_=WQP[n // 32, :, (n % 32) * 256:(n % 32 + 1) * 256]),
                           "qwld0", 16)

            @blk.scalar
            def _(s):
                def qodd(n):
                    sm.inc(s.dma_start(
                        out=wqs[1 + (n // 2) % 2][:, :],
                        in_=WQP[n // 32, :, (n % 32) * 256:(n % 32 + 1) * 256]),
                           "qwld1", 16)

                qodd(1)
                for g2 in range(16):
                    for k in range(32):
                        n = g2 * 32 + k
                        s.wait_ge(sm[f"qwld{n % 2}"],
                                  b[f"qwld{n % 2}"] + 16 * (n // 2 + 1))
                        if n >= 2:
                            s.wait_ge(sm["qwu"], b["qwu"] + n - 1)
                        qsrc = wqs[0] if n % 2 == 0 else wqs[1 + (n // 2) % 2]
                        sm.inc(s.activation(wqb[n % 2][:, :], qsrc[:, :],
                                            AF.Copy), "qwc")
                        if n + 2 < 512 and (n + 2) % 2 == 1:
                            qodd(n + 2)
                    s.wait_ge(sm["qwu"], b["qwu"] + 32 * (g2 + 1))
                    for m2 in range(2):
                        hh = 2 * g2 + m2
                        for h2 in range(nh2):
                            c0 = hh * OC + h2 * 512
                            last = s.activation(q_sb[:, c0:c0 + 512],
                                                qpi(g2 % par, m2, h2)[:, :],
                                                AF.Copy, scale=SCALE)
                    sm.inc(last, "qev")

            @blk.tensor
            def _(t):
                t.wait_ge(sm["qyld0"], b["qyld0"] + 16 * 16)
                t.wait_ge(sm["qyld1"], b["qyld1"] + 16 * 16)
                for g2 in range(16):
                    if g2 >= par:
                        t.wait_ge(sm["qev"], b["qev"] + g2 - par + 1)
                    for k in range(32):
                        n = g2 * 32 + k
                        t.wait_ge(sm["qwc"], b["qwc"] + n + 1)
                        for m2 in range(2):
                            for h2 in range(nh2):
                                last = t.matmul(
                                    qpi(g2 % par, m2, h2)[:, :],
                                    wqb[n % 2][:, m2 * 128:(m2 + 1) * 128],
                                    y2s[:, k * OC + h2 * 512:k * OC + h2 * 512 + 512],
                                    start=(k == 0), stop=(k == 31))
                        sm.inc(last, "qwu")
    return


def _emit_wo(nc, gb, sm, att, WOP, X1SRC, XB, so_abs, OC):
    nh2 = OC // 512
    par = 2 if nh2 <= 2 else 1
    q_sb, wqs, wqb, xsw = att["q_sb"], att["wqs"], att["wqb"], att["xsw"]
    with ExitStack() as pse:
        op = [pse.enter_context(nc.psum_tensor(f"wop{i}" + _uid(), [128, 512], F32))
              for i in range(par * 2 * nh2)]
        b = {k: sm.n[k] for k in ("wwld0", "wwld1", "wwc", "wwu", "wxld0",
                                  "wxld1", "wxadd", "wst20", "wst21")}

        def opi(p, m2, h2):
            return op[(2 * nh2) * p + nh2 * m2 + h2]

        with gb.Block() as blk:
            @blk.sync
            def _(s):
                for n in range(0, 512, 2):
                    if n >= 2:
                        s.wait_ge(sm["wwc"], b["wwc"] + n - 1)
                    sm.inc(s.dma_start(
                        out=wqs[0][:, :],
                        in_=WOP[n // 32, :, (n % 32) * 256:(n % 32 + 1) * 256]),
                           "wwld0", 16)

            @blk.scalar
            def _(s):
                def wodd(n):
                    sm.inc(s.dma_start(
                        out=wqs[1 + (n // 2) % 2][:, :],
                        in_=WOP[n // 32, :, (n % 32) * 256:(n % 32 + 1) * 256]),
                           "wwld1", 16)

                wodd(1)
                for g2 in range(16):
                    for k in range(32):
                        n = g2 * 32 + k
                        s.wait_ge(sm[f"wwld{n % 2}"],
                                  b[f"wwld{n % 2}"] + 16 * (n // 2 + 1))
                        if n >= 2:
                            s.wait_ge(sm["wwu"], b["wwu"] + n - 1)
                        wsrc = wqs[0] if n % 2 == 0 else wqs[1 + (n // 2) % 2]
                        sm.inc(s.activation(wqb[n % 2][:, :], wsrc[:, :],
                                            AF.Copy), "wwc")
                        if n + 2 < 512 and (n + 2) % 2 == 1:
                            wodd(n + 2)

            @blk.tensor
            def _(t):
                for g2 in range(16):
                    if g2 >= par:
                        t.wait_ge(sm["wxadd"], b["wxadd"] + 2 * (g2 - par + 1))
                    for k in range(32):
                        n = g2 * 32 + k
                        t.wait_ge(sm["wwc"], b["wwc"] + n + 1)
                        for m2 in range(2):
                            for h2 in range(nh2):
                                last = t.matmul(
                                    opi(g2 % par, m2, h2)[:, :],
                                    wqb[n % 2][:, m2 * 128:(m2 + 1) * 128],
                                    q_sb[:, k * OC + h2 * 512:k * OC + h2 * 512 + 512],
                                    start=(k == 0), stop=(k == 31))
                        sm.inc(last, "wwu")

            @blk.gpsimd
            def _(g):
                for om in range(32):
                    if om == 0:
                        g.wait_ge(sm["wst20"], b["wst20"])
                        g.wait_ge(sm["wst21"], b["wst21"])
                    if om >= 2:
                        g.wait_ge(sm[f"wst2{om % 2}"],
                                  b[f"wst2{om % 2}"] + 16 * (om // 2))
                    sm.inc(g.dma_start(out=xsw[om % 2][:, 0:OC],
                                       in_=X1SRC[om, :, so_abs:so_abs + OC]),
                           f"wxld{om % 2}", 16)
                    g.wait_ge(sm["wxadd"], b["wxadd"] + om + 1)
                    sm.inc(g.dma_start(out=XB[om, :, so_abs:so_abs + OC],
                                       in_=xsw[om % 2][:, 0:OC]),
                           f"wst2{om % 2}", 16)

            @blk.vector
            def _(v):
                for om in range(32):
                    g2, m2 = om // 2, om % 2
                    v.wait_ge(sm["wwu"], b["wwu"] + 32 * (g2 + 1))
                    v.wait_ge(sm[f"wxld{om % 2}"],
                              b[f"wxld{om % 2}"] + 16 * (om // 2 + 1))
                    for h2 in range(nh2):
                        last = v.tensor_add(xsw[om % 2][:, h2 * 512:(h2 + 1) * 512],
                                            xsw[om % 2][:, h2 * 512:(h2 + 1) * 512],
                                            opi(g2 % par, m2, h2)[:, :])
                    sm.inc(last, "wxadd")


def _emit_vkeys(nc, gb, sm, att, P):
    nkt_all = P // 128
    kt_sb, vt_sb, vk_sb, idnb = att["kt"], att["vt"], att["vk"], att["idnb"]
    KVB = att["KVB"]
    with ExitStack() as pse:
        tp = [pse.enter_context(nc.psum_tensor(f"vtp{i}" + _uid(), [128, 128], BF16))
              for i in range(2)]
        b_l1 = sm.n["qyld1"]
        b_tr = sm.n["vktr"]; b_ev = sm.n["vkev"]
        with gb.Block() as blk:
            @blk.gpsimd
            def _(g):
                sm.inc(g.dma_start(out=kt_sb[:, 0:P], in_=KVB[0, :, 0:P]),
                       "qyld0", 16)
                sm.inc(g.dma_start(out=vt_sb[:, 0:P], in_=KVB[1, :, 0:P]),
                       "qyld1", 16)

            @blk.tensor
            def _(t):
                t.wait_ge(sm["qyld1"], b_l1 + 16)
                for n in range(nkt_all):
                    if n >= 2:
                        t.wait_ge(sm["vkev"], b_ev + n - 1)
                    sm.inc(t.transpose(tp[n % 2][:, :],
                                       vt_sb[:, n * 128:(n + 1) * 128],
                                       idnb[:, :]), "vktr")

            @blk.vector
            def _(v2):
                for n in range(nkt_all):
                    v2.wait_ge(sm["vktr"], b_tr + n + 1)
                    sm.inc(v2.tensor_copy(vk_sb[:, n * 128:(n + 1) * 128],
                                          tp[n % 2][:, :]), "vkev")


def _emit_kmax(nc, gb, sm, att, P):
    kt_sb, qsq, nrow, bsc = att["kt"], att["qsq"], att["nrow"], att["bsc"]
    onesb = att["onesb"]
    chs = []
    o = 0
    while o < P:
        w = min(1024, P - o)
        chs.append((o, w))
        o += w
    nch = len(chs)
    mc = [w // 512 for (_, w) in chs]
    cmm = [0]
    for m in mc:
        cmm.append(cmm[-1] + m)
    cev = [0]
    for m in mc:
        cev.append(cev[-1] + m + 1)
    with ExitStack() as pse:
        knp = pse.enter_context(nc.psum_tensor("knp" + _uid(), [1, 512], F32))
        b_sq = sm.n["knsq"]; b_mm = sm.n["knmm"]; b_ev = sm.n["knev"]
        kt_rdy = sm.n["qyld0"]
        with gb.Block() as blk:
            @blk.scalar
            def _(s):
                s.wait_ge(sm["qyld0"], kt_rdy)
                for ch, (o2, w) in enumerate(chs):
                    if ch >= 1:
                        s.wait_ge(sm["knmm"], b_mm + cmm[ch])
                    sm.inc(s.activation(qsq[:, 0:w], kt_sb[:, o2:o2 + w],
                                        AF.Square), "knsq")

            @blk.tensor
            def _(t):
                prev = None
                for ch in range(nch):
                    t.wait_ge(sm["knsq"], b_sq + ch + 1)
                    for h2 in range(mc[ch]):
                        if prev is not None:
                            t.wait_ge(sm["knev"], b_ev + prev)
                        sm.inc(t.matmul(knp[:, :], onesb[:, :],
                                        qsq[:, h2 * 512:(h2 + 1) * 512],
                                        start=True, stop=True), "knmm")
                        prev = cev[ch] + h2 + 1

            @blk.vector
            def _(v2):
                for ch, (o2, w) in enumerate(chs):
                    for h2 in range(mc[ch]):
                        v2.wait_ge(sm["knmm"], b_mm + cmm[ch] + h2 + 1)
                        if h2 == 0 and ch >= 1:
                            v2.wait_ge(sm["knev"], b_ev + cev[ch])
                        sm.inc(v2.tensor_copy(nrow[:, h2 * 512:(h2 + 1) * 512],
                                              knp[:, :]), "knev")
                    v2.wait_ge(sm["knev"], b_ev + cev[ch] + mc[ch])
                    sm.inc(v2.reduce_max(bsc[:, 64 + ch:65 + ch], nrow[:, 0:w],
                                         axis=AX), "knev")
                v2.wait_ge(sm["knev"], b_ev + cev[nch])
                sm.inc(v2.reduce_max(bsc[:, 0:1], bsc[:, 64:64 + nch], axis=AX),
                       "knev")


def _emit_norms(nc, gb, sm, att, vj):
    v = VM[vj]
    OC, nsc = v["O"], v["nsc"]
    nq = OC // 512
    q_sb, qsq, nrow, bsc = att["q_sb"], att["qsq"], att["nrow"], att["bsc"]
    bcol, ba, otr = att["bcol"], att["ba"], att["otr"]
    onesb, onesfr = att["onesb"], att["onesfr"]
    with ExitStack() as pse:
        np1 = pse.enter_context(nc.psum_tensor("np1" + _uid(), [1, 512], F32))
        bbp = pse.enter_context(nc.psum_tensor("bbp" + _uid(), [128, 32], F32))
        batp = [pse.enter_context(nc.psum_tensor(f"bat{i}" + _uid(),
                                                 [128, NSC_MAX], F32))
                for i in range(2)]
        b_sq = sm.n["qnsq"]; b_mm = sm.n["qnmm"]; b_ev = sm.n["qnev"]
        b_bc = sm.n["bch"]; b_br = sm.n["bldr"]; b_bv = sm.n["bldv"]
        with gb.Block() as blk:
            @blk.scalar
            def _(s):
                for h in range(NH):
                    if h >= 1:
                        s.wait_ge(sm["qnmm"], b_mm + nq * h)
                    sm.inc(s.activation(qsq[:, 0:OC], q_sb[:, h * OC:(h + 1) * OC],
                                        AF.Square), "qnsq")
                s.wait_ge(sm["bch"], b_bc + 1)
                sm.inc(s.activation(bsc[:, 32:64], bsc[:, 32:64], AF.Sqrt),
                       "bch")
                s.wait_ge(sm["bch"], b_bc + 3)
                sm.inc(s.activation(bcol[:, :], bbp[:, :], AF.Copy), "bch")

            @blk.tensor
            def _(t):
                for h in range(NH):
                    t.wait_ge(sm["qnsq"], b_sq + h + 1)
                    for h2 in range(nq):
                        if h2 > 0:
                            t.wait_ge(sm["qnev"], b_ev + (nq + 1) * h + h2)
                        elif h >= 1:
                            t.wait_ge(sm["qnev"], b_ev + (nq + 1) * h - 1)
                        sm.inc(t.matmul(np1[:, :], onesb[:, :],
                                        qsq[:, h2 * 512:(h2 + 1) * 512],
                                        start=True, stop=True), "qnmm")
                t.wait_ge(sm["bch"], b_bc + 2)
                sm.inc(t.matmul(bbp[:, :], onesfr[:, :], bsc[:, 32:64],
                                start=True, stop=True), "bch")
                for h in range(NH):
                    if h >= 2:
                        t.wait_ge(sm["bldv"], b_bv + h - 1)
                    sm.inc(t.matmul(batp[h % 2][:, 0:nsc], onesfr[:, :],
                                    otr[0:1, h * NSC_MAX:h * NSC_MAX + nsc],
                                    start=True, stop=True), "bldr")

            @blk.vector
            def _(v2):
                for h in range(NH):
                    for h2 in range(nq):
                        v2.wait_ge(sm["qnmm"], b_mm + nq * h + h2 + 1)
                        if h2 == 0 and h >= 1:
                            v2.wait_ge(sm["qnev"], b_ev + (nq + 1) * h)
                        sm.inc(v2.tensor_copy(nrow[:, h2 * 512:(h2 + 1) * 512],
                                              np1[:, :]), "qnev")
                    v2.wait_ge(sm["qnev"], b_ev + (nq + 1) * h + nq)
                    sm.inc(v2.reduce_max(bsc[:, 1 + h:2 + h], nrow[:, 0:OC],
                                         axis=AX), "qnev")
                v2.wait_ge(sm["qnev"], b_ev + (nq + 1) * NH)
                sm.inc(v2.tensor_scalar(bsc[:, 32:64], bsc[:, 1:33],
                                        bsc[:, 0:1], None, ALU.mult), "bch")
                v2.wait_ge(sm["bch"], b_bc + 4)
                for h in range(NH):
                    v2.wait_ge(sm["bldr"], b_br + h + 1)
                    sm.inc(v2.tensor_scalar(ba[:, h * NSC_MAX:h * NSC_MAX + nsc],
                                            batp[h % 2][:, 0:nsc],
                                            bcol[:, h:h + 1], None,
                                            ALU.subtract), "bldv")


def _emit_einsum(nc, gb, sm, att, vj):
    v = VM[vj]
    tiles, ext, nsc, scidx = v["tiles"], v["ext"], v["nsc"], v["scidx"]
    OC, t0 = v["O"], v["t0"]
    nslots = len(tiles)
    kt_sb, vk_sb, q_sb = att["kt"], att["vk"], att["q_sb"]
    r0_sb, mk4_sb, idnb, idnf = att["r0"], att["mk4"], att["idnb"], att["idnf"]
    onesfr = att["onesfr"]
    relh_sb, rm4_sb = att["relh"], att["rm4"]
    sc2_sb, pr_sb, pt_sb = att["sc2"], att["pr"], att["pt"]
    sums_sb, ssum_sb, srow_sb, rb_sb = (att["sums"], att["ssum"], att["srow"],
                                        att["rb"])
    ba = att["ba"]

    chunks = []
    slot_of = {}
    g = 0
    for h in range(NH):
        for s in range(nslots):
            for c in range(ext[s]):
                chunks.append((h, s, c))
                if c == ext[s] - 1:
                    slot_of[g] = (h, s, h * nslots + s)
                g += 1
    NG = g

    with ExitStack() as pse:
        scp = [pse.enter_context(nc.psum_tensor(f"scp{i}" + _uid(), [128, 512], F32))
               for i in range(2)]
        ptp = [pse.enter_context(nc.psum_tensor(f"ptp{i}" + _uid(), [128, 512], BF16))
               for i in range(2)]
        pvp = [pse.enter_context(nc.psum_tensor(f"pvp{i}" + _uid(), [128, 128], F32))
               for i in range(2)]
        nrp = pse.enter_context(nc.psum_tensor("nrp" + _uid(), [1, 128], F32))
        rbp = pse.enter_context(nc.psum_tensor("rbp" + _uid(), [128, 128], F32))
        b_scmm = sm.n["scmm"]; b_scv = sm.n["scv"]
        b_sexp = sm.n["sexp"]; b_strc = sm.n["strc"]
        b_ptev = sm.n["ptev"]; b_spv = sm.n["spv"]
        b_nsum = sm.n["nsum"]; b_nrow1 = sm.n["nrow1"]
        b_nrec = sm.n["nrec"]; b_nrb2 = sm.n["nrb2"]
        b_nrbc = sm.n["nrbc"]; b_ndone = sm.n["ndone"]
        with gb.Block() as blk:
            @blk.tensor
            def _(t):
                for gg in range(NG + 6):
                    if gg < NG:
                        h, s, c = chunks[gg]
                        if gg >= 2:
                            t.wait_ge(sm["scv"], b_scv + gg - 1)
                        tp_ = (tiles[s] - t0) * 128
                        sm.inc(t.matmul(
                            scp[gg % 2][:, :],
                            q_sb[:, h * OC + tp_:h * OC + tp_ + 128],
                            kt_sb[:, c * 512:(c + 1) * 512],
                            start=True, stop=True), "scmm")
                    gp = gg - 2
                    if 0 <= gp < NG:
                        t.wait_ge(sm["sexp"], b_sexp + gp + 1)
                        if gp >= 2:
                            t.wait_ge(sm["ptev"], b_ptev + gp - 1)
                        for bbk in range(4):
                            last = t.transpose(
                                ptp[gp % 2][:, bbk * 128:(bbk + 1) * 128],
                                pr_sb[gp % 4][:, bbk * 128:(bbk + 1) * 128],
                                idnb[:, :])
                        sm.inc(last, "strc")
                        if gp in slot_of:
                            h2_, s2_, sl = slot_of[gp]
                            t.wait_ge(sm["ptev"], b_ptev + gp + 1)
                            if sl >= 2:
                                t.wait_ge(sm["ndone"], b_ndone + sl - 1)
                            nkt = 4 * ext[s2_]
                            for kt in range(nkt):
                                last = t.matmul(pvp[sl % 2][:, :],
                                                vk_sb[:, kt * 128:(kt + 1) * 128],
                                                pt_sb[:, kt * 128:(kt + 1) * 128],
                                                start=(kt == 0),
                                                stop=(kt == nkt - 1))
                            sm.inc(last, "spv")
                            t.wait_ge(sm["nsum"], b_nsum + sl + 1)
                            if sl >= 1:
                                t.wait_ge(sm["nrec"], b_nrec + sl)
                            sm.inc(t.matmul(nrp[:, :], ssum_sb[:, :], idnf[:, :],
                                            start=True, stop=True), "nrow1")
                    gm = gg - 3
                    if 0 <= gm < NG and gm in slot_of:
                        _h, _s, sl = slot_of[gm]
                        t.wait_ge(sm["nrec"], b_nrec + sl + 1)
                        if sl >= 1:
                            t.wait_ge(sm["nrbc"], b_nrbc + sl)
                        sm.inc(t.matmul(rbp[:, :], onesfr[:, :], srow_sb[:, :],
                                        start=True, stop=True), "nrb2")

            @blk.vector
            def _(v2):
                for gg in range(NG + 6):
                    if gg < NG:
                        h, s, c = chunks[gg]
                        if s == 0 and c == 0:
                            if h >= 1:
                                v2.wait_ge(sm["scv"], b_scv + nsc * h)
                            for m in range(4):
                                v2.scalar_tensor_tensor(
                                    rm4_sb[:, m * 512:(m + 1) * 512], r0_sb[:, :],
                                    float(SLOPES[h]),
                                    mk4_sb[:, m * 512:(m + 1) * 512],
                                    ALU.mult, ALU.add)
                            sm.inc(v2.tensor_scalar(relh_sb[:, :], r0_sb[:, :],
                                                    float(SLOPES[h]), None,
                                                    ALU.mult), "relw")
                            v2.wait_ge(sm["relw"], sm.n["relw"])
                        v2.wait_ge(sm["scmm"], b_scmm + gg + 1)
                        if gg >= 2:
                            v2.wait_ge(sm["sexp"], b_sexp + gg - 1)
                        diag = (c == ext[s] - 1)
                        rel = (rm4_sb[:, (tiles[s] % 4) * 512:
                                      (tiles[s] % 4 + 1) * 512]
                               if diag else relh_sb[:, :])
                        sm.inc(v2.scalar_tensor_tensor(
                            sc2_sb[gg % 2][:, :], scp[gg % 2][:, :], 1.0,
                            rel, ALU.mult, ALU.add), "scv")
                    gp = gg - 2
                    if 0 <= gp < NG:
                        h2_, s2_, c2_ = chunks[gp]
                        v2.wait_ge(sm["strc"], b_strc + gp + 1)
                        if c2_ == 0 and (h2_ * nslots + s2_) >= 1:
                            v2.wait_ge(sm["spv"], b_spv + h2_ * nslots + s2_)
                        sm.inc(v2.tensor_copy(pt_sb[:, c2_ * 512:(c2_ + 1) * 512],
                                              ptp[gp % 2][:, :]), "ptev")
                        if gp in slot_of:
                            _h, _s, sl = slot_of[gp]
                            v2.wait_ge(sm["sexp"], b_sexp + gp + 1)
                            if sl >= 1:
                                v2.wait_ge(sm["nrow1"], b_nrow1 + sl)
                            i0 = (h2_ % 2) * NSC_MAX + scidx[(s2_, 0)]
                            sm.inc(v2.reduce_sum(ssum_sb[:, :],
                                                 sums_sb[:, i0:i0 + ext[s2_]],
                                                 axis=AX), "nsum")
                    gr = gg - 3
                    if 0 <= gr < NG and gr in slot_of:
                        _h, _s, sl = slot_of[gr]
                        v2.wait_ge(sm["nrow1"], b_nrow1 + sl + 1)
                        if sl >= 1:
                            v2.wait_ge(sm["nrb2"], b_nrb2 + sl)
                        sm.inc(v2.reciprocal(srow_sb[:, :], nrp[:, :]), "nrec")
                    gf = gg - 5
                    if 0 <= gf < NG and gf in slot_of:
                        h2_, s2_, sl = slot_of[gf]
                        v2.wait_ge(sm["nrbc"], b_nrbc + sl + 1)
                        v2.wait_ge(sm["spv"], b_spv + sl + 1)
                        c0 = h2_ * OC + (tiles[s2_] - t0) * 128
                        sm.inc(v2.tensor_mul(q_sb[:, c0:c0 + 128],
                                             pvp[sl % 2][:, :], rb_sb[:, :]),
                               "ndone")

            @blk.scalar
            def _(s_):
                for gg in range(NG + 6):
                    if gg < NG:
                        h, s, c = chunks[gg]
                        if s == 0 and c == 0 and h >= 2:
                            s_.wait_ge(sm["nsum"], b_nsum + nslots * (h - 1))
                        s_.wait_ge(sm["scv"], b_scv + gg + 1)
                        if gg >= 4:
                            s_.wait_ge(sm["strc"], b_strc + gg - 3)
                        idx = scidx[(s, c)]
                        scol = (h % 2) * NSC_MAX + idx
                        sm.inc(s_.activation(
                            pr_sb[gg % 4][:, :], sc2_sb[gg % 2][:, :], AF.Exp,
                            bias=ba[:, h * NSC_MAX + idx:h * NSC_MAX + idx + 1],
                            accum_out=sums_sb[:, scol:scol + 1]), "sexp")
                    gb_ = gg - 4
                    if 0 <= gb_ < NG and gb_ in slot_of:
                        _h, _s, sl = slot_of[gb_]
                        s_.wait_ge(sm["nrb2"], b_nrb2 + sl + 1)
                        if sl >= 1:
                            s_.wait_ge(sm["ndone"], b_ndone + sl)
                        sm.inc(s_.activation(rb_sb[:, :], rbp[:, :], AF.Copy),
                               "nrbc")



ENGS = ["sync", "gpsimd", "vector", "scalar", "tensor"]


def build_fused():
    nc = bass.Bass("TRN2", target_bir_lowering=False)
    X = nc.dram_tensor("x", [NPT, 128, S], F32, kind="ExternalInput")
    U1 = nc.dram_tensor("u1p", [8, 128, 4096], F32, kind="ExternalInput")
    V1U2 = nc.dram_tensor("v1u2p", [86, 128, 3072], F32, kind="ExternalInput")
    V2 = nc.dram_tensor("v2p", [32, 128, 1024], F32, kind="ExternalInput")
    WQP = nc.dram_tensor("wqp", [16, 128, 8192], F32, kind="ExternalInput")
    WOP = nc.dram_tensor("wop", [16, 128, 8192], F32, kind="ExternalInput")
    WKVP = nc.dram_tensor("wkvp", [128, 8192], F32, kind="ExternalInput")
    B1P = nc.dram_tensor("b1p", [128, 172], F32, kind="ExternalInput")
    B2P = nc.dram_tensor("b2p", [128, 32], F32, kind="ExternalInput")
    R0 = nc.dram_tensor("r0", [128, 512], F32, kind="ExternalInput")
    MK4 = nc.dram_tensor("mask4", [128, 2048], BF16, kind="ExternalInput")
    IDB = nc.dram_tensor("idn_bf", [128, 128], BF16, kind="ExternalInput")
    IDF = nc.dram_tensor("idn_f32", [128, 128], F32, kind="ExternalInput")
    ONB = nc.dram_tensor("ones_bf", [128, 1], BF16, kind="ExternalInput")
    ONF = nc.dram_tensor("ones_f32r", [1, 128], F32, kind="ExternalInput")
    OTR = nc.dram_tensor("otr", [1, NH * NSC_MAX], F32, kind="ExternalInput")
    VID = nc.dram_tensor("vid", [1, 4], I32, kind="ExternalInput")
    XB = nc.dram_tensor("xb", [NPT, 128, S], F32, kind="Internal")
    YB = nc.dram_tensor("yb", [NPT, 128, S], BF16, kind="Internal")
    KVB = nc.dram_tensor("kvb", [2, 128, S], BF16, kind="Internal")
    OUT = nc.dram_tensor("out", [NPT, 128, OMAX], F32, kind="ExternalOutput")
    wd = {"u1p": U1, "v1u2p": V1U2, "v2p": V2}

    with ExitStack() as es0:
        sm = Sems(nc, es0, ["i0"] + LN_SEMS + FFN_SEMS + KV_SEMS + ATT_SEMS)
        sb0 = lambda n, sh, dt: es0.enter_context(nc.sbuf_tensor(n, sh, dt))
        vid_sb = sb0("vid_sb", [1, 4], I32)
        b1p_sb = sb0("b1ps", [128, 172], F32)
        b2p_sb = sb0("b2ps", [128, 32], F32)
        b2h_sb = sb0("b2hs", [128, 32], F32)
        onesb = sb0("onesb", [128, 1], BF16)
        onesfr = sb0("onesfr", [1, 128], F32)

        vregs = {en: {} for en in ENGS}

        with nc.Block() as blk:
            @blk.gpsimd
            def _(g):
                for dst, src in ((vid_sb, VID), (b1p_sb, B1P), (b2p_sb, B2P),
                                 (onesb, ONB), (onesfr, ONF)):
                    sm.inc(g.dma_start(out=dst[:, :], in_=src[:, :]), "i0", 16)

            def mkbody(en):
                def body(e):
                    e.wait_ge(sm["i0"], 80)
                    for key, col in (("vid", 0), ("g1", 1), ("g2", 2)):
                        r = e.alloc_register(f"{en}_{key}")
                        e.reg_load(r, vid_sb[0:1, col:col + 1])
                        vregs[en][key] = r
                getattr(blk, en)(body)

            for en in ENGS:
                mkbody(en)

            @blk.vector
            def _(v):
                v.tensor_scalar(b2h_sb[:, :], b2p_sb[:, :], 0.5, None, ALU.mult)

        def fill_common(bufs):
            bufs["b1p"] = b1p_sb
            bufs["b2h"] = b2h_sb
            bufs["ones_bf"] = onesb
            bufs["ones_f32r"] = onesfr

        def emit_prefix_chunk(gg, bufs, y_sb, p_sb, zacc, so, T, first):
            pre = (None if first else
                   [("fst0", sm.n["fst0"]), ("fst1", sm.n["fst1"])])
            _emit_ln(nc, gg, sm, bufs, X, y_sb, T, so, pre=pre)
            _emit_ffn(nc, gg, sm, bufs, y_sb, p_sb, zacc, X, XB, wd, T, so, so,
                      False)
            pre2 = [("fst0", sm.n["fst0"]), ("fst1", sm.n["fst1"])]
            _emit_ln(nc, gg, sm, bufs, XB, y_sb, T, so, pre=pre2)
            _emit_kv(nc, gg, sm, bufs, y_sb, WKVP, KVB, YB, T, so)

        with ExitStack() as esP:
            bufs = _common_bufs(nc, esP, "p_")
            fill_common(bufs)
            y_sb = esP.enter_context(nc.sbuf_tensor("p_y", [128, NPT * 1024], BF16))
            p_sb = esP.enter_context(nc.sbuf_tensor("p_p", [128, 8192], BF16))
            zacc = esP.enter_context(nc.sbuf_tensor("p_z", [128, 8192], F32))
            emit_prefix_chunk(GGate(nc, vregs, None, None), bufs, y_sb, p_sb,
                              zacc, 0, 1024, True)
            s1 = sm.snap()
            emit_prefix_chunk(GGate(nc, vregs, "g1", 1), bufs, y_sb, p_sb,
                              zacc, 1024, 1024, False)
            s2 = sm.snap()
            emit_prefix_chunk(GGate(nc, vregs, "g2", 1), bufs, y_sb, p_sb,
                              zacc, 2048, 1024, False)
            s3 = sm.snap()
            emit_prefix_chunk(GGate(nc, vregs, "vid", 3), bufs, y_sb, p_sb,
                              zacc, 3072, 1024, False)
            s4 = sm.snap()
            sm.restore(s1)
            emit_prefix_chunk(GGate(nc, vregs, "vid", 0), bufs, y_sb, p_sb,
                              zacc, 1024, 512, False)
            e0 = sm.snap()
            sm.restore(s2)
            emit_prefix_chunk(GGate(nc, vregs, "vid", 1), bufs, y_sb, p_sb,
                              zacc, 2048, 512, False)
            e1 = sm.snap()
            sm.restore(s3)
            emit_prefix_chunk(GGate(nc, vregs, "vid", 2), bufs, y_sb, p_sb,
                              zacc, 3072, 512, False)
            e2 = sm.snap()
        ends = [e0, e1, e2, s4]

        for j in range(4):
            sm.restore(ends[j])
            gj = GGate(nc, vregs, "vid", j)
            v = VM[j]
            O, P = v["O"], v["P"]
            pfx = f"a{j}_"
            with ExitStack() as esO:
                sbo = lambda n, sh, dt: esO.enter_context(
                    nc.sbuf_tensor(pfx + n, sh, dt))
                att = {
                    "KVB": KVB,
                    "kt": sbo("kt", [128, P], BF16),
                    "vk": sbo("vk", [128, P], BF16),
                    "q_sb": sbo("q", [128, NH * O], BF16),
                    "wqs": [sbo(f"wqs{i}", [128, 256], F32) for i in range(3)],
                    "wqb": [sbo(f"wqb{i}", [128, 256], BF16) for i in range(2)],
                    "idnb": sbo("idnb", [128, 128], BF16),
                    "onesb": onesb, "onesfr": onesfr,
                }
                b_cl = sm.n["qwld0"]
                with gj.Block() as blk:
                    @blk.sync
                    def _(s):
                        sm.inc(s.dma_start(out=att["idnb"][:, :], in_=IDB[:, :]),
                               "qwld0", 16)
                        s.wait_ge(sm["qwld0"], b_cl + 16)
                sm.n["qwld0"] = b_cl + 16
                with ExitStack() as esA:
                    att["vt"] = esA.enter_context(
                        nc.sbuf_tensor(pfx + "vt", [128, P], BF16))
                    att["y2s"] = esA.enter_context(
                        nc.sbuf_tensor(pfx + "y2s", [128, NPT * O], BF16))
                    _emit_vkeys(nc, gj, sm, att, P)
                    _emit_qproj(nc, gj, sm, att, WQP, YB, BKS[j], O)
                with ExitStack() as esB:
                    sbb = lambda n, sh, dt: esB.enter_context(
                        nc.sbuf_tensor(pfx + n, sh, dt))
                    att["sums"] = sbb("sums", [128, 2 * NSC_MAX], F32)
                    att["ssum"] = sbb("ssum", [128, 1], F32)
                    att["srow"] = sbb("srow", [1, 128], F32)
                    att["rb"] = sbb("rb", [128, 128], F32)
                    att["bsc"] = sbb("bsc", [1, 96], F32)
                    att["bcol"] = sbb("bcol", [128, 32], F32)
                    att["ba"] = sbb("ba", [128, NH * NSC_MAX], F32)
                    att["r0"] = sbb("r0", [128, 512], F32)
                    att["mk4"] = sbb("mk4", [128, 2048], BF16)
                    att["idnf"] = sbb("idnf", [128, 128], F32)
                    att["otr"] = sbb("otr", [1, NH * NSC_MAX], F32)
                    att["qsq"] = sbb("qsq", [128, max(O, 1024)], BF16)
                    att["nrow"] = sbb("nrow", [1, 1536], F32)
                    att["sc2"] = [sbb(f"sc2{i}", [128, 512], F32)
                                  for i in range(2)]
                    att["pr"] = [sbb(f"pr{i}", [128, 512], BF16)
                                 for i in range(4)]
                    att["pt"] = sbb("pt", [128, 512 * max(v["ext"])], BF16)
                    att["relh"] = sbb("relh", [128, 512], F32)
                    att["rm4"] = sbb("rm4", [128, 2048], F32)
                    b_c2 = sm.n["qwld1"]
                    with gj.Block() as blk:
                        @blk.sync
                        def _(s):
                            for dst, src in ((att["r0"], R0), (att["mk4"], MK4),
                                             (att["idnf"], IDF),
                                             (att["otr"], OTR)):
                                sm.inc(s.dma_start(out=dst[:, :], in_=src[:, :]),
                                       "qwld1", 16)
                            s.wait_ge(sm["qwld1"], b_c2 + 64)
                    sm.n["qwld1"] = b_c2 + 64
                    _emit_kmax(nc, gj, sm, att, P)
                    _emit_norms(nc, gj, sm, att, j)
                    _emit_einsum(nc, gj, sm, att, j)
                with ExitStack() as esC:
                    att["xsw"] = [esC.enter_context(
                        nc.sbuf_tensor(pfx + f"xsw{i}", [128, O], F32))
                        for i in range(2)]
                    _emit_wo(nc, gj, sm, att, WOP, XB, XB, BKS[j], O)
            with ExitStack() as esF:
                bufs2 = _common_bufs(nc, esF, f"f{j}_")
                fill_common(bufs2)
                y2 = esF.enter_context(
                    nc.sbuf_tensor(f"f{j}_y", [128, NPT * 1024], BF16))
                p2 = esF.enter_context(
                    nc.sbuf_tensor(f"f{j}_p", [128, 8192], BF16))
                z2 = esF.enter_context(
                    nc.sbuf_tensor(f"f{j}_z", [128, 8192], F32))
                for (so, T) in _chunks_of(BKS[j], BKS[j + 1]):
                    pre = [("wst20", sm.n["wst20"]), ("wst21", sm.n["wst21"]),
                           ("fst0", sm.n["fst0"]), ("fst1", sm.n["fst1"])]
                    _emit_ln(nc, gj, sm, bufs2, XB, y2, T, so, pre=pre)
                    _emit_ffn(nc, gj, sm, bufs2, y2, p2, z2, XB, OUT, wd, T,
                              so, so - BKS[j], True)
    return nc



def kernel(**inputs):
    x = np.asarray(inputs["L0_attn_hidden_states"], np.float32)
    wp = _pack_weights(
        np.asarray(inputs["L0_attn_Wq"], np.float32),
        np.asarray(inputs["L0_attn_Wkv"], np.float32),
        np.asarray(inputs["L0_attn_Wo"], np.float32),
        np.asarray(inputs["L0_ffn_U1"], np.float32),
        np.asarray(inputs["L0_ffn_V1"], np.float32),
        np.asarray(inputs["L0_ffn_U2"], np.float32),
        np.asarray(inputs["L0_ffn_V2"], np.float32),
        np.asarray(inputs["L0_ffn_b1"], np.float32),
        np.asarray(inputs["L0_ffn_b2"], np.float32))
    cs = _consts_shared()
    nc = build_fused()
    ins = []
    for c in range(8):
        b, j = c // 4, c % 4
        xT = np.ascontiguousarray(x[b].T.reshape(NPT, 128, S))
        vid = np.array([[j, 1 if j >= 1 else 0, 1 if j >= 2 else 0, 0]],
                       np.int32)
        d = {"x": xT, "vid": vid, "otr": _otab_row(j),
             "u1p": wp["u1p"], "v1u2p": wp["v1u2p"], "v2p": wp["v2p"],
             "wqp": wp["wqp"], "wop": wp["wop"], "wkvp": wp["wkvp"],
             "b1p": wp["b1p"], "b2p": wp["b2p"],
             "r0": cs["r0"], "mask4": cs["mask4"],
             "idn_bf": cs["idn_bf"], "idn_f32": cs["idn_f32"],
             "ones_bf": cs["ones_bf"], "ones_f32r": cs["ones_f32r"]}
        ins.append(d)
    res = run_bass_kernel_spmd(nc, ins, core_ids=list(range(8)))
    out = np.empty((NB, S, D), np.float32)
    for c in range(8):
        b, j = c // 4, c % 4
        O = BKS[j + 1] - BKS[j]
        a = np.asarray(res.results[c]["out"])
        out[b, BKS[j]:BKS[j + 1], :] = a[:, :, 0:O].reshape(D, O).T
    return out
